# version 10
# baseline (speedup 1.0000x reference)
import numpy as np
from contextlib import ExitStack

import concourse.bass as bass
import concourse.mybir as mybir
from concourse.bass_utils import run_bass_kernel_spmd

F32 = mybir.dt.float32
BF16 = mybir.dt.bfloat16
I8 = mybir.dt.int8
AF = mybir.ActivationFunctionType
ALU = mybir.AluOpType
AXX = mybir.AxisListType.X

D = 4096
KT = 32
H = 8
CW = 256
NCH = D // CW
TOK = 2048
HALF = TOK // 2
NTILE = HALF // 128
NSEQ = HALF // 512

ENGS = ("sync", "gpsimd", "scalar", "vector", "tensor")


def build_program(stop='full', nowo=False, nowomm=False):
    nc = bass.Bass("TRN2", target_bir_lowering=False)

    x_in = nc.dram_tensor("x_in", [TOK, D], F32, kind="ExternalInput")
    wqT = nc.dram_tensor("wqT", [D, H * 128], F32, kind="ExternalInput")
    bq2 = nc.dram_tensor("bq2", [H, 128], F32, kind="ExternalInput")
    m_in = nc.dram_tensor("m_in", [H, 128, 128], F32, kind="ExternalInput")
    z_in = nc.dram_tensor("z_in", [H, 128], F32, kind="ExternalInput")
    woTb = nc.dram_tensor("woTb", [4, H * 128, D], F32, kind="ExternalInput")
    w8 = nc.dram_tensor("w8", [D, D], I8, kind="ExternalInput")
    wsc = nc.dram_tensor("wsc", [D, D], F32, kind="ExternalInput")
    bff = nc.dram_tensor("bff", [1, D], F32, kind="ExternalInput")
    ident = nc.dram_tensor("ident", [128, 128], F32, kind="ExternalInput")
    out = nc.dram_tensor("out", [TOK, D], F32, kind="ExternalOutput")
    woT_s = nc.dram_tensor("woT_s", [H, 128, D], BF16, kind="Internal")
    wqT_s = nc.dram_tensor("wqT_s", [KT, 128, H * 128], BF16, kind="Internal")

    with ExitStack() as es:
        def sbuf(name, shape, dt):
            return es.enter_context(nc.sbuf_tensor(name, shape, dt))

        xnT = sbuf("xnT", [128, KT * HALF], BF16)
        retrT = sbuf("retrT", [128, H * HALF], BF16)
        m_bf = sbuf("m_bf", [128, H * 128], BF16)
        zrep = sbuf("zrep", [128, H * 128], BF16)
        b_rep = sbuf("b_rep", [128, D], BF16)
        ident_bf = sbuf("ident_bf", [128, 128], BF16)
        ones_sq = sbuf("ones_sq", [128, 128], BF16)
        onesf = sbuf("onesf", [128, 128], F32)
        z_g = [sbuf(f"z_g{g}", [128, 1], F32) for g in range(H)]
        bq_g = [sbuf(f"bq_g{g}", [128, 1], F32) for g in range(H)]
        c_eps5 = sbuf("c_eps5", [128, 1], F32)
        c_eps6 = sbuf("c_eps6", [128, 1], F32)
        c_invd = sbuf("c_invd", [128, 1], F32)
        c_neg1 = sbuf("c_neg1", [128, 1], F32)
        c_one = sbuf("c_one", [128, 1], F32)
        c_i128 = sbuf("c_i128", [128, 1], F32)
        c_junk = sbuf("c_junk", [128, 1], F32)
        c_nsum = sbuf("c_nsum", [128, 1], F32)
        c_nm = sbuf("c_nm", [128, 1], F32)
        c_ssq = sbuf("c_ssq", [128, 1], F32)
        c_nm2n = sbuf("c_nm2n", [128, 1], F32)
        c_var = sbuf("c_var", [128, 1], F32)
        c_sig = sbuf("c_sig", [128, 1], F32)
        c_inv = sbuf("c_inv", [128, 1], F32)

        ps_q = [es.enter_context(nc.psum_tensor(f"ps_q{i}", [128, 512], F32)) for i in range(2)]
        ps_n = [es.enter_context(nc.psum_tensor(f"ps_n{i}", [128, 512], F32)) for i in range(2)]
        ps_b = es.enter_context(nc.psum_tensor("ps_b", [128, 512], F32))
        ps_d = es.enter_context(nc.psum_tensor("ps_d", [128, 512], F32))
        ps_tr = [es.enter_context(nc.psum_tensor(f"ps_tr{i}", [128, 128], F32)) for i in range(2)]

        sem_names = ["sxt0", "sxt1", "swq0", "swq1", "sw80", "sw81", "ssc0", "ssc1", "ssc2", "ssc3",
                     "sxr0", "sxr1", "sout0", "sout1", "sacc0", "sacc1", "sacc2", "sacc3", "swos",
                     "swot0", "swot1", "scst", "swqs", "swqc",
                     "e_g", "e_ln", "e_lnr", "e_vtr", "e_ttr",
                     "e_tq", "e_sg", "e_vsig", "e_tn", "e_vr", "e_tb", "e_vretr",
                     "e_dqc", "e_dqp", "e_vwoc", "e_ev", "e_tf", "ph"]
        S = {n: es.enter_context(nc.semaphore(n)) for n in sem_names}
        C = {n: 0 for n in sem_names}

        n_xt = n_wq = n_w8 = n_sc = n_xr = n_out = n_acc = 0

        block = es.enter_context(nc.Block())

        def emit(ops):
            for eng in ENGS:
                lst = ops.get(eng)
                if not lst:
                    continue
                getattr(block, eng)(lambda h, lst=lst: [f(h) for f in lst])

        def mkops():
            return {e: [] for e in ENGS}

        def barrier():
            ops = mkops()
            C["ph"] += 5
            tgt = C["ph"]
            pre = {
                "tensor": [("e_vretr", C["e_vretr"]), ("e_ev", C["e_ev"]),
                           ("e_vtr", C["e_vtr"]), ("e_vsig", C["e_vsig"])],
                "gpsimd": [("sout0", 16 * ((n_out + 1) // 2)), ("sout1", 16 * (n_out // 2)),
                           ("sacc0", C["sacc0"]), ("sacc1", C["sacc1"]),
                           ("sacc2", C["sacc2"]), ("sacc3", C["sacc3"]), ("swos", C["swos"])],
                "vector": [("e_tf", C["e_tf"])],
                "scalar": [],
                "sync": [],
            }
            for eng in ENGS:
                def f(h, t=tgt, waits=tuple(pre[eng])):
                    for sname, val in waits:
                        if val:
                            h.wait_ge(S[sname], val)
                    h.nop().then_inc(S["ph"], 1)
                    h.wait_ge(S["ph"], t)
                ops[eng].append(f)
            emit(ops)

        sg_of, tn_of, vr_of, retr_of = {}, {}, {}, {}

        if stop == 'bar':
            barrier()
            return nc
        es_pro = ExitStack()
        bcol = [es_pro.enter_context(nc.sbuf_tensor(f"bcol{j}", [128, 1], F32))
                for j in range(KT)]
        rep_f = es_pro.enter_context(nc.sbuf_tensor("rep_f", [128, 128], F32))
        rep_b = es_pro.enter_context(nc.sbuf_tensor("rep_b", [128, 128], BF16))
        ops = mkops()

        def g_pro(g):
            g.memset(ones_sq[:, :], 1.0)
            g.memset(onesf[:, :], 1.0)
            g.memset(c_eps5[:, :], 1e-5)
            g.memset(c_eps6[:, :], 1e-6)
            g.memset(c_invd[:, :], 1.0 / D)
            g.memset(c_neg1[:, :], -1.0)
            g.memset(c_one[:, :], 1.0)
            g.memset(c_i128[:, :], 1.0 / 128.0)
            g.memset(c_junk[:, :], 0.0).then_inc(S["e_g"], 1)
            g.dma_start(out=m_bf[:, :].rearrange("p (g e) -> p g e", g=H),
                        in_=m_in.rearrange("g d e -> d g e")).then_inc(S["scst"], 16)
            g.dma_start(out=ident_bf[:, :], in_=ident[:, :]).then_inc(S["scst"], 16)
            for j in range(KT):
                g.dma_start(out=bcol[j][:, :],
                            in_=bff.rearrange("o f -> f o")[j * 128:(j + 1) * 128, :],
                            allow_slow_non_contiguous=True).then_inc(S["scst"], 16)
            for gg in range(H):
                g.dma_start(out=z_g[gg][:, :], in_=z_in.rearrange("g d -> d g")[:, gg:gg + 1],
                            allow_slow_non_contiguous=True).then_inc(S["scst"], 16)
                g.dma_start(out=bq_g[gg][:, :], in_=bq2.rearrange("g d -> d g")[:, gg:gg + 1],
                            allow_slow_non_contiguous=True).then_inc(S["scst"], 16)
        ops["gpsimd"].append(g_pro)
        C["e_g"] += 1
        C["scst"] += 16 * (2 + KT + 2 * H)
        cst_done = C["scst"]
        if stop == 'pro_g':
            emit(ops)
            es_pro.close()
            return nc

        def s_pro(s, cst_done=cst_done):
            s.wait_ge(S["e_g"], 1)
            s.wait_ge(S["scst"], cst_done)
            ins = None
            for gg in range(H):
                ins = s.activation(zrep[:, gg * 128:(gg + 1) * 128], onesf[:, :], AF.Copy,
                                   scale=z_g[gg][:, 0:1])
            ins.then_inc(S["e_sg"], 1)
        ops["scalar"].append(s_pro)
        C["e_sg"] += 1
        if stop == 'pro_s':
            emit(ops)
            es_pro.close()
            return nc

        def v_pro(v, cst_done=cst_done):
            v.wait_ge(S["e_g"], 1)
            v.wait_ge(S["scst"], cst_done)
            ins = None
            for j in range(KT):
                ins = v.tensor_scalar(bcol[j][:, :], bcol[j][:, :], c_i128[:, 0:1], None, ALU.mult)
            ins.then_inc(S["e_ln"], 1)
        ops["vector"].append(v_pro)
        C["e_ln"] += 1
        if stop == 'pro_v':
            emit(ops)
            es_pro.close()
            return nc

        for j in range(KT):
            def s_bj(s, j=j, v_need=C["e_vsig"] + 0, ln_need=C["e_ln"]):
                if j == 0:
                    s.wait_ge(S["e_ln"], ln_need)
                else:
                    s.wait_ge(S["e_vsig"], v_need)
                s.activation(rep_f[:, :], onesf[:, :], AF.Copy,
                             scale=bcol[j][:, 0:1]).then_inc(S["e_sg"], 1)
            ops["scalar"].append(s_bj)
            C["e_sg"] += 1

            def v_bj(v, j=j, s_need=C["e_sg"], t_need=C["e_tb"]):
                v.wait_ge(S["e_sg"], s_need)
                if j >= 1:
                    v.wait_ge(S["e_tb"], t_need)
                v.tensor_copy(rep_b[:, :], rep_f[:, :]).then_inc(S["e_vsig"], 1)
            ops["vector"].append(v_bj)
            C["e_vsig"] += 1

            def t_bj(t, j=j, v_need=C["e_vsig"], e_need=C["e_vretr"]):
                if j == 0:
                    t.wait_ge(S["e_g"], 1)
                t.wait_ge(S["e_vsig"], v_need)
                if j >= 1:
                    t.wait_ge(S["e_vretr"], e_need)
                t.matmul(ps_b[:, 0:128], rep_b[:, :], ident_bf[:, :],
                         start=True, stop=True).then_inc(S["e_tb"], 1)
            ops["tensor"].append(t_bj)
            C["e_tb"] += 1

            def v_bj2(v, j=j, t_need=C["e_tb"]):
                v.wait_ge(S["e_tb"], t_need)
                v.tensor_copy(b_rep[:, j * 128:(j + 1) * 128], ps_b[:, 0:128]).then_inc(S["e_vretr"], 1)
            ops["vector"].append(v_bj2)
            C["e_vretr"] += 1

        emit(ops)
        es_pro.close()
        barrier()

        halves = [] if stop == 'pro' else ([0] if stop in ('ln0', 'a0', 'b0') else [0, 1])
        for half in halves:
            with ExitStack() as esA:
                def sbA(name, shape, dt):
                    return esA.enter_context(nc.sbuf_tensor(f"{name}_h{half}", shape, dt))
                x_t = [sbA(f"x_t{i}", [128, D], F32) for i in range(2)]
                xsq = sbA("xsq", [128, D], BF16)
                xn = [sbA(f"xn{i}", [128, D], BF16) for i in range(2)]
                wq_b = [sbA(f"wq_b{i}", [128, KT * 128], BF16) for i in range(2)]
                esb = sbA("esb", [128, 512], F32)
                rsb = sbA("rsb", [128, 512], F32)
                sig_bf = [sbA(f"sig_bf{i}", [128, 512], BF16) for i in range(2)]
                dnf = sbA("dnf", [128, 512], F32)
                if half == 0:
                    woraw = sbA("woraw", [128, D], F32)
                    wo_stage = sbA("wo_stage", [128, D], BF16)
                rcp_f = sbA("rcp_f", [128, 512], F32)
                denb = sbA("denb", [128, 512], BF16)

                ops = mkops()
                ln_base = C["e_ln"]
                lnr_base = C["e_lnr"]
                vtr_base = C["e_vtr"]
                ttr_base = C["e_ttr"]

                for tloc in range(NTILE):
                    T = half * NTILE + tloc

                    def f_x(sy, T=T, tloc=tloc, need=ln_base + tloc - 1):
                        if tloc >= 2:
                            sy.wait_ge(S["e_ln"], need)
                        sy.dma_start(out=x_t[tloc % 2][:, :],
                                     in_=x_in[T * 128:(T + 1) * 128, :]).then_inc(S[f"sxt{T % 2}"], 16)
                    ops["sync"].append(f_x)
                    n_xt += 1

                for tloc in range(NTILE):
                    Tg = half * NTILE + tloc
                    xt_sem = f"sxt{Tg % 2}"
                    xt_need = 16 * (Tg // 2 + 1)

                    def s_sq(s, tloc=tloc, xt_sem=xt_sem, xt_need=xt_need, lnr_need=lnr_base + tloc):
                        s.wait_ge(S[xt_sem], xt_need)
                        if tloc >= 1:
                            s.wait_ge(S["e_lnr"], lnr_need)
                        s.activation(xsq[:, :], x_t[tloc % 2][:, :],
                                     AF.Square).then_inc(S["e_sg"], 1)
                    ops["scalar"].append(s_sq)
                    C["e_sg"] += 1
                    sq_done = C["e_sg"]

                    def v_st(v, tloc=tloc, xt_sem=xt_sem, xt_need=xt_need, sq_done=sq_done):
                        v.wait_ge(S[xt_sem], xt_need)
                        xt = x_t[tloc % 2]
                        v.tensor_reduce(c_nsum[:, :], xt[:, :], AXX, ALU.add, negate=True)
                        v.tensor_scalar(c_nm[:, :], c_nsum[:, :], c_invd[:, 0:1], None, ALU.mult)
                        v.wait_ge(S["e_sg"], sq_done)
                        v.tensor_reduce(c_ssq[:, :], xsq[:, :], AXX, ALU.add).then_inc(S["e_lnr"], 1)
                        v.tensor_scalar(c_nm2n[:, :], c_nm[:, :], c_nm[:, 0:1], c_neg1[:, 0:1],
                                        ALU.mult, ALU.mult)
                        v.tensor_scalar(c_ssq[:, :], c_ssq[:, :], c_invd[:, 0:1], None, ALU.mult)
                        v.tensor_copy(c_junk[:, :], c_one[:, :])
                        v.tensor_scalar(c_var[:, :], c_ssq[:, :], c_nm2n[:, 0:1],
                                        None, ALU.add).then_inc(S["e_vr"], 1)
                    ops["vector"].append(v_st)
                    C["e_lnr"] += 1
                    C["e_vr"] += 1
                    var_done = C["e_vr"]

                    def s_rt(s, var_done=var_done):
                        s.wait_ge(S["e_vr"], var_done)
                        s.activation(c_sig[:, :], c_var[:, :], AF.Sqrt,
                                     bias=c_eps5[:, 0:1]).then_inc(S["e_sg"], 1)
                    ops["scalar"].append(s_rt)
                    C["e_sg"] += 1
                    sig_done = C["e_sg"]

                    def v_xn(v, tloc=tloc, sig_done=sig_done,
                             ttr_need=ttr_base + 32 * (tloc - 1)):
                        v.wait_ge(S["e_sg"], sig_done)
                        v.reciprocal(c_inv[:, :], c_sig[:, :])
                        v.tensor_copy(c_junk[:, :], c_one[:, :])
                        if tloc >= 2:
                            v.wait_ge(S["e_ttr"], ttr_need)
                        v.tensor_scalar(xn[tloc % 2][:, :], x_t[tloc % 2][:, :],
                                        c_nm[:, 0:1], c_inv[:, 0:1],
                                        ALU.add, ALU.mult).then_inc(S["e_ln"], 1)
                    ops["vector"].append(v_xn)
                    C["e_ln"] += 1
                    xn_done = C["e_ln"]

                    def t_tr(t, tloc=tloc, xn_done=xn_done, vtr_b=vtr_base + 32 * tloc,
                             first=(half == 0 and tloc == 0)):
                        t.wait_ge(S["e_ln"], xn_done)
                        for k in range(KT):
                            gk = vtr_b - vtr_base + k
                            if gk >= 2:
                                t.wait_ge(S["e_vtr"], vtr_b + k - 1)
                            t.matmul(ps_tr[k % 2][:, :], xn[tloc % 2][:, k * 128:(k + 1) * 128],
                                     ident_bf[:, :], start=True,
                                     stop=True).then_inc(S["e_ttr"], 1)
                    ops["tensor"].append(t_tr)
                    C["e_ttr"] += KT

                    def v_tr(v, tloc=tloc, ttr_b=ttr_base + 32 * tloc):
                        for k in range(KT):
                            v.wait_ge(S["e_ttr"], ttr_b + k + 1)
                            v.tensor_copy(
                                xnT[:, k * HALF + tloc * 128: k * HALF + (tloc + 1) * 128],
                                ps_tr[k % 2][:, :]).then_inc(S["e_vtr"], 1)
                    ops["vector"].append(v_tr)
                    C["e_vtr"] += KT

                all_tr = C["e_vtr"]
                base_tq = C["e_tq"]
                do_attn = stop not in ('ln0',)
                a_stage = 5
                if stop.startswith('a1h'):
                    a_stage = int(stop[4:])

                mlist = list(range(H)) if do_attn else []
                if stop.startswith('a1h'):
                    mlist = [0]
                wq_cnt_of = {}

                def emit_wq(m):
                    def f_wq(sy, m=m, tq_need=base_tq + 2 * (m - 1)):
                        if m >= 2:
                            sy.wait_ge(S["e_tq"], tq_need)
                        else:
                            sy.wait_ge(S["swqs"], 128)
                        sy.dma_start(
                            out=wq_b[m % 2][:, :].rearrange("p (k q) -> p k q", k=KT),
                            in_=wqT_s.rearrange("k p q -> p k q")[:, :, m * 128:(m + 1) * 128],
                        ).then_inc(S[f"swq{m % 2}"], 16)
                    ops["sync"].append(f_wq)

                def emit_wo_iter(i):
                    q, g_i = i // 8, i % 8
                    slot = i % 4
                    sem = f"sacc{slot}"
                    cols = slice(slot * (D // 4), (slot + 1) * (D // 4))
                    scols = slice(q * (D // 4), (q + 1) * (D // 4))
                    for r in range(4):
                        def f_acc(g, q=q, g_i=g_i, r=r, sem=sem, cols=cols, scols=scols,
                                  prev=C[sem]):
                            if prev:
                                g.wait_ge(S[sem], prev)
                            srcw = woTb.rearrange("r (g p) f -> r p g f", p=128)[r][:, g_i, scols]
                            if r == 0:
                                g.dma_start(out=woraw[:, cols], in_=srcw).then_inc(S[sem], 16)
                            else:
                                g.dma_start(out=woraw[:, cols], in_=srcw,
                                            accum_op=ALU.add).then_inc(S[sem], 16)
                        ops["gpsimd"].append(f_acc)
                        C[sem] += 16

                    def f_cast(g, i=i, sem=sem, cols=cols, need=C[sem],
                               wos_need=16 * (i - 3)):
                        if i < 4:
                            g.wait_ge(S["swqs"], 128)
                        else:
                            g.wait_ge(S["swos"], wos_need)
                        g.wait_ge(S[sem], need)
                        g.tensor_copy(wo_stage[:, cols], woraw[:, cols])
                    ops["gpsimd"].append(f_cast)

                    def f_wos(g, q=q, g_i=g_i, cols=cols, scols=scols):
                        g.dma_start(out=woT_s[g_i][:, scols],
                                    in_=wo_stage[:, cols]).then_inc(S["swos"], 16)
                    ops["gpsimd"].append(f_wos)
                    C["swos"] += 16

                if mlist:
                    if half == 0:
                        def f_strips(g):
                            for s in range(8):
                                if s >= 1:
                                    g.wait_ge(S["swqs"], 16 * s)
                                g.dma_start(
                                    out=wo_stage[:, :].rearrange("p (k q) -> p k q", k=4),
                                    in_=wqT.rearrange("(s k p) q -> s p k q",
                                                      s=8, p=128)[s],
                                ).then_inc(S["swqc"], 16)
                                g.wait_ge(S["swqc"], 16 * (s + 1))
                                g.dma_start(
                                    out=wqT_s[4 * s:4 * s + 4].rearrange("k p q -> p k q"),
                                    in_=wo_stage[:, :].rearrange("p (k q) -> p k q", k=4),
                                ).then_inc(S["swqs"], 16)
                        ops["gpsimd"].append(f_strips)
                        C["swqc"] += 128
                        C["swqs"] += 128
                        for i in range(32):
                            emit_wo_iter(i)
                    for m in mlist:
                        emit_wq(m)
                    for m in mlist:
                        n_wq += 1
                        wq_cnt_of[m] = (f"swq{m % 2}", 16 * ((half * H + m) // 2 + 1))

                for m in mlist:
                    for n in (range(NSEQ) if not stop.startswith('a1h') else [0]):
                        s_idx = C["e_tq"]

                        def t_q(t, m=m, n=n, s_idx=s_idx, all_tr=all_tr, base_tq=base_tq,
                                wq_need=wq_cnt_of[m]):
                            if m == 0 and n == 0:
                                t.wait_ge(S["e_vtr"], all_tr)
                            t.wait_ge(S[wq_need[0]], wq_need[1])
                            if s_idx - 2 >= base_tq:
                                t.wait_ge(S["e_sg"], sg_of[s_idx - 2])
                            ins = None
                            for k in range(KT):
                                ins = t.matmul(ps_q[n][:, :], wq_b[m % 2][:, k * 128:(k + 1) * 128],
                                               xnT[:, k * HALF + n * 512: k * HALF + (n + 1) * 512],
                                               start=(k == 0), stop=(k == KT - 1))
                            ins.then_inc(S["e_tq"], 1)
                        ops["tensor"].append(t_q)
                        C["e_tq"] += 1
                        if a_stage < 2:
                            continue
                        tq_done = C["e_tq"]

                        def s_sig(s, m=m, n=n, tq_done=tq_done):
                            s.wait_ge(S["e_tq"], tq_done)
                            s.activation(esb[:, :], ps_q[n][:, :], AF.Exp, bias=bq_g[m][:, 0:1])
                            s.activation(rsb[:, :], ps_q[n][:, :], AF.Relu,
                                         bias=bq_g[m][:, 0:1]).then_inc(S["e_sg"], 1)
                        ops["scalar"].append(s_sig)
                        C["e_sg"] += 1
                        sg_of[s_idx] = C["e_sg"]
                        if a_stage < 3:
                            continue

                        def v_sig(v, n=n, s_idx=s_idx, base_tq=base_tq, ready=C["e_sg"]):
                            v.wait_ge(S["e_sg"], ready)
                            if s_idx - 2 >= base_tq:
                                v.wait_ge(S["e_tn"], tn_of[s_idx - 2])
                            v.tensor_scalar(esb[:, :], esb[:, :], c_one[:, 0:1], None, ALU.min)
                            v.tensor_copy(c_junk[:, :], c_one[:, :])
                            v.tensor_tensor(sig_bf[n][:, :], rsb[:, :], esb[:, :],
                                            ALU.add).then_inc(S["e_vsig"], 1)
                        ops["vector"].append(v_sig)
                        C["e_vsig"] += 1
                        if a_stage < 4:
                            continue

                        def t_nd(t, m=m, n=n, s_idx=s_idx, base_tq=base_tq,
                                 vsig_done=C["e_vsig"]):
                            t.wait_ge(S["e_vsig"], vsig_done)
                            if s_idx - 1 >= base_tq:
                                t.wait_ge(S["e_vretr"], retr_of[s_idx - 1])
                            t.matmul(ps_n[n][:, :], m_bf[:, m * 128:(m + 1) * 128],
                                     sig_bf[n][:, :], start=True, stop=True)
                            t.matmul(ps_d[:, :], zrep[:, m * 128:(m + 1) * 128],
                                     sig_bf[n][:, :], start=True,
                                     stop=True).then_inc(S["e_tn"], 1)
                        ops["tensor"].append(t_nd)
                        C["e_tn"] += 1
                        tn_of[s_idx] = C["e_tn"]
                        if a_stage < 5:
                            continue

                        def v_retr(v, m=m, n=n, nd_done=C["e_tn"]):
                            v.wait_ge(S["e_tn"], nd_done)
                            v.tensor_scalar(dnf[:, :], ps_d[:, :], c_eps6[:, 0:1],
                                            None, ALU.add)
                            v.tensor_copy(c_junk[:, :], c_one[:, :])
                            v.reciprocal(rcp_f[:, :], dnf[:, :])
                            v.tensor_copy(c_junk[:, :], c_one[:, :])
                            v.tensor_copy(denb[:, :], rcp_f[:, :])
                            v.tensor_copy(c_junk[:, :], c_one[:, :])
                            v.tensor_tensor(
                                retrT[:, m * HALF + n * 512: m * HALF + (n + 1) * 512],
                                ps_n[n][:, :], denb[:, :], ALU.mult).then_inc(S["e_vretr"], 1)
                        ops["vector"].append(v_retr)
                        C["e_vretr"] += 1
                        retr_of[s_idx] = C["e_vretr"]

                emit(ops)
            barrier()
            if stop in ('ln0', 'a0') or stop.startswith('a1h'):
                continue

            with ExitStack() as esB:
                def sbB(name, shape, dt):
                    return esB.enter_context(nc.sbuf_tensor(f"{name}_h{half}", shape, dt))
                wdq = [sbB(f"wdq{i}", [128, KT * CW], BF16) for i in range(2)]
                w8b = [sbB(f"w8b{i}", [128, KT * CW], I8) for i in range(2)]
                scb = [sbB(f"scb{i}", [128, 4 * CW], F32) for i in range(4)]
                wobf = [sbB(f"wobf{i}", [128, H * CW], BF16) for i in range(2)]
                xrb = [sbB(f"xrb{i}", [128, NTILE * CW], F32) for i in range(2)]
                ost = [sbB(f"ost{i}", [128, NTILE * CW], F32) for i in range(2)]

                ops = mkops()
                tf_base = C["e_tf"]
                ev_base = C["e_ev"]
                dqc_base = C["e_dqp"] // 8
                out_base = n_out
                xr_base = n_xr
                retr_ready = C["e_vretr"]

                def emit_evacs(v, c, half=half, ev_cnt_base=ev_base, tf_b=tf_base,
                               xr_b=xr_base, out_b=out_base):
                    cg = half * NCH + c
                    for tt in range(NTILE):
                        v.wait_ge(S["e_tf"], tf_b + c * NTILE + tt + 1)
                        if tt == 0:
                            v.wait_ge(S[f"sxr{c % 2}"], 16 * (cg // 2 + 1))
                            if c >= 2:
                                v.wait_ge(S[f"sout{c % 2}"], 16 * (cg // 2))
                        v.tensor_tensor(ost[c % 2][:, tt * CW:(tt + 1) * CW],
                                        ps_q[tt % 2][:, 0:CW],
                                        xrb[c % 2][:, tt * CW:(tt + 1) * CW],
                                        ALU.add).then_inc(S["e_ev"], 1)

                for c in range(NCH):
                    cc0 = c * CW

                    def sy_w8(sy, c=c, cc0=cc0, dq_need=8 * (dqc_base + c - 1)):
                        if c >= 2:
                            sy.wait_ge(S["e_dqp"], dq_need)
                        sy.dma_start(
                            out=w8b[c % 2][:, :].rearrange("p (k j) -> p k j", k=KT),
                            in_=w8.rearrange("(k p) f -> p k f", p=128)[:, :, cc0:cc0 + CW],
                        ).then_inc(S[f"sw8{c % 2}"], 16)
                    ops["sync"].append(sy_w8)
                    n_w8 += 1

                    for kg in range(8):
                        def sy_sc(sy, c=c, cc0=cc0, kg=kg, pidx=n_sc):
                            if pidx >= 4:
                                sy.wait_ge(S["e_dqp"], pidx - 3)
                            sy.dma_start(
                                out=scb[pidx % 4][:, :].rearrange("p (kk j) -> p kk j", kk=4),
                                in_=wsc.rearrange("(kg kk p) f -> kg p kk f",
                                                  kg=8, p=128)[kg][:, :, cc0:cc0 + CW],
                            ).then_inc(S[f"ssc{pidx % 4}"], 16)
                        ops["sync"].append(sy_sc)
                        n_sc += 1

                    def sy_xr(sy, c=c, cc0=cc0, half=half, ev_need=ev_base + 8 * (c - 1)):
                        if c >= 2:
                            sy.wait_ge(S["e_ev"], ev_need)
                        sy.dma_start(
                            out=xrb[c % 2][:, :].rearrange("p (tt j) -> p tt j", tt=NTILE),
                            in_=x_in.rearrange("(hh tt p) f -> hh tt p f", hh=2, p=128)[half]
                                    .rearrange("tt p f -> p tt f")[:, :, cc0:cc0 + CW],
                        ).then_inc(S[f"sxr{c % 2}"], 16)
                    ops["sync"].append(sy_xr)
                    n_xr += 1

                    def sy_wot(sy, c=c, cc0=cc0, half=half, tf_need=tf_base + 8 * (c - 1)):
                        if half == 0:
                            sy.wait_ge(S["swos"], 16 * 8 * (c // 4 + 1))
                        if c >= 2:
                            sy.wait_ge(S["e_tf"], tf_need)
                        sy.dma_start(
                            out=wobf[c % 2][:, :].rearrange("p (g j) -> p g j", g=H),
                            in_=woT_s.rearrange("g p f -> p g f")[:, :, cc0:cc0 + CW],
                        ).then_inc(S[f"swot{c % 2}"], 16)
                    if not nowo:
                        ops["gpsimd"].append(sy_wot)

                    if c >= 1:
                        def g_out(g, cprev=c - 1, half=half, ev_need=ev_base + 8 * c,
                                  so=n_out):
                            g.wait_ge(S["e_ev"], ev_need)
                            if so >= 2:
                                g.wait_ge(S[f"sout{so % 2}"], 16 * (so // 2))
                            g.dma_start(
                                out=out.rearrange("(hh tt p) f -> hh tt p f", hh=2, p=128)[half]
                                       .rearrange("tt p f -> p tt f")[:, :, cprev * CW:(cprev + 1) * CW],
                                in_=ost[cprev % 2][:, :].rearrange("p (tt j) -> p tt j", tt=NTILE),
                            ).then_inc(S[f"sout{so % 2}"], 16)
                        ops["gpsimd"].append(g_out)
                        n_out += 1

                    def v_dq(v, c=c, w8_sem=f"sw8{c % 2}", w8_need=16 * ((half * NCH + c) // 2 + 1),
                             piece0=n_sc - 8, tf_need=tf_base + 8 * (c - 1)):
                        v.wait_ge(S[w8_sem], w8_need)
                        if c >= 2:
                            v.wait_ge(S["e_tf"], tf_need)
                        for k in range(KT):
                            kg = k // 4
                            if k % 4 == 0:
                                pp = piece0 + kg
                                v.wait_ge(S[f"ssc{pp % 4}"], 16 * (pp // 4 + 1))
                            pidx = piece0 + kg
                            ins = v.tensor_tensor(wdq[c % 2][:, k * CW:(k + 1) * CW],
                                                  w8b[c % 2][:, k * CW:(k + 1) * CW],
                                                  scb[pidx % 4][:, (k % 4) * CW:(k % 4 + 1) * CW],
                                                  ALU.mult)
                            if k % 4 == 3:
                                ins.then_inc(S["e_dqp"], 1)
                    ops["vector"].append(v_dq)
                    C["e_dqp"] += 8

                    if c >= 1:
                        def v_ev(v, cprev=c - 1):
                            emit_evacs(v, cprev)
                        ops["vector"].append(v_ev)
                        C["e_ev"] += NTILE

                    def t_ch(t, c=c, dq_need=8 * (dqc_base + c + 1),
                             wo_sem=f"swot{c % 2}", wo_need=16 * ((half * NCH + c) // 2 + 1),
                             ev_b=ev_base, retr_ready=retr_ready):
                        if c == 0:
                            t.wait_ge(S["e_vretr"], retr_ready)
                        t.wait_ge(S["e_dqp"], dq_need)
                        if not nowo:
                            t.wait_ge(S[wo_sem], wo_need)
                        for tt in range(NTILE):
                            gi = c * NTILE + tt
                            if gi >= 2:
                                t.wait_ge(S["e_ev"], ev_b + gi - 1)
                            psq = ps_q[tt % 2]
                            for k in range(KT):
                                t.matmul(psq[:, 0:CW],
                                         xnT[:, k * HALF + tt * 128: k * HALF + (tt + 1) * 128],
                                         wdq[c % 2][:, k * CW:(k + 1) * CW],
                                         start=(k == 0), stop=False)
                            if not (nowo or nowomm):
                                for g in range(H):
                                    t.matmul(psq[:, 0:CW],
                                             retrT[:, g * HALF + tt * 128: g * HALF + (tt + 1) * 128],
                                             wobf[c % 2][:, g * CW:(g + 1) * CW],
                                             start=False, stop=False)
                            t.matmul(psq[:, 0:CW], ones_sq[:, :], b_rep[:, c * CW:(c + 1) * CW],
                                     start=False, stop=True).then_inc(S["e_tf"], 1)
                    ops["tensor"].append(t_ch)
                    C["e_tf"] += NTILE

                def v_ev_last(v, cprev=NCH - 1):
                    emit_evacs(v, cprev)
                ops["vector"].append(v_ev_last)
                C["e_ev"] += NTILE

                def g_out_last(g, cprev=NCH - 1, half=half, ev_need=ev_base + 8 * NCH,
                               so=n_out):
                    g.wait_ge(S["e_ev"], ev_need)
                    if so >= 2:
                        g.wait_ge(S[f"sout{so % 2}"], 16 * (so // 2))
                    g.dma_start(
                        out=out.rearrange("(hh tt p) f -> hh tt p f", hh=2, p=128)[half]
                               .rearrange("tt p f -> p tt f")[:, :, cprev * CW:(cprev + 1) * CW],
                        in_=ost[cprev % 2][:, :].rearrange("p (tt j) -> p tt j", tt=NTILE),
                    ).then_inc(S[f"sout{so % 2}"], 16)
                ops["gpsimd"].append(g_out_last)
                n_out += 1

                emit(ops)
            barrier()

        if n_out:
            ops = mkops()
            t0v = 16 * ((n_out + 1) // 2)
            t1v = 16 * (n_out // 2)
            def g_fin(g, t0v=t0v, t1v=t1v):
                g.wait_ge(S["sout0"], t0v)
                g.wait_ge(S["sout1"], t1v)
            ops["gpsimd"].append(g_fin)
            emit(ops)

    return nc


_NC_CACHE = {}


def kernel(L0_attn_hidden_states, L0_attn_Wq, L0_attn_bq, L0_attn_Wk, L0_attn_bk,
           L0_attn_Wv, L0_attn_bv, L0_attn_Wo, L0_attn_M, L0_attn_z,
           L0_ffn_w, L0_ffn_w_scale, L0_ffn_b):
    x = np.ascontiguousarray(np.asarray(L0_attn_hidden_states, dtype=np.float32))
    Wq = np.asarray(L0_attn_Wq, dtype=np.float32)
    bq = np.asarray(L0_attn_bq, dtype=np.float32)
    Wo = np.asarray(L0_attn_Wo, dtype=np.float32)
    M = np.ascontiguousarray(np.asarray(L0_attn_M, dtype=np.float32))
    z = np.ascontiguousarray(np.asarray(L0_attn_z, dtype=np.float32))
    w8 = np.ascontiguousarray(np.asarray(L0_ffn_w, dtype=np.int8))
    wsc = np.ascontiguousarray(np.asarray(L0_ffn_w_scale, dtype=np.float32))
    bff = np.ascontiguousarray(np.asarray(L0_ffn_b, dtype=np.float32).reshape(1, D))

    Wq_sel = Wq.reshape(8, 4, 128, D)[:, 0].reshape(8 * 128, D)
    wqT = np.ascontiguousarray(Wq_sel.T)
    bq2 = np.ascontiguousarray(bq.reshape(8, 4, 128)[:, 0])
    woTb = np.ascontiguousarray(
        Wo.T.reshape(8, 4, 128, D).transpose(1, 0, 2, 3).reshape(4, 1024, D))
    ident = np.eye(128, dtype=np.float32)

    if "nc" not in _NC_CACHE:
        _NC_CACHE["nc"] = build_program()
    nc = _NC_CACHE["nc"]

    in_maps = []
    for i in range(8):
        in_maps.append(dict(x_in=np.ascontiguousarray(x[i]), wqT=wqT, bq2=bq2,
                            m_in=M, z_in=z, woTb=woTb, w8=w8, wsc=wsc, bff=bff,
                            ident=ident))
    res = run_bass_kernel_spmd(nc, in_maps, core_ids=list(range(8)))
    return np.stack([r["out"] for r in res.results], axis=0)


# version 11
# speedup vs baseline: 1.1108x; 1.1108x over previous
import numpy as np
from contextlib import ExitStack

import concourse.bass as bass
import concourse.mybir as mybir
from concourse.bass_utils import run_bass_kernel_spmd

F32 = mybir.dt.float32
BF16 = mybir.dt.bfloat16
I8 = mybir.dt.int8
AF = mybir.ActivationFunctionType
ALU = mybir.AluOpType
AXX = mybir.AxisListType.X

D = 4096
KT = 32
H = 8
CW = 256
NCH = D // CW
TOK = 2048
HALF = TOK // 2
NTILE = HALF // 128
NSEQ = HALF // 512

ENGS = ("sync", "gpsimd", "scalar", "vector", "tensor")


def build_program(stop='full', nowo=False, nowomm=False):
    nc = bass.Bass("TRN2", target_bir_lowering=False)

    x_in = nc.dram_tensor("x_in", [TOK, D], F32, kind="ExternalInput")
    wqT = nc.dram_tensor("wqT", [D, H * 128], F32, kind="ExternalInput")
    bq2 = nc.dram_tensor("bq2", [H, 128], F32, kind="ExternalInput")
    m_in = nc.dram_tensor("m_in", [H, 128, 128], F32, kind="ExternalInput")
    z_in = nc.dram_tensor("z_in", [H, 128], F32, kind="ExternalInput")
    woTb = nc.dram_tensor("woTb", [4, H * 128, D], F32, kind="ExternalInput")
    w8 = nc.dram_tensor("w8", [D, D], I8, kind="ExternalInput")
    wsc = nc.dram_tensor("wsc", [D, D], F32, kind="ExternalInput")
    bff = nc.dram_tensor("bff", [1, D], F32, kind="ExternalInput")
    ident = nc.dram_tensor("ident", [128, 128], F32, kind="ExternalInput")
    out = nc.dram_tensor("out", [TOK, D], F32, kind="ExternalOutput")
    woT_s = nc.dram_tensor("woT_s", [H, 128, D], BF16, kind="Internal")
    wqT_s = nc.dram_tensor("wqT_s", [KT, 128, H * 128], BF16, kind="Internal")

    with ExitStack() as es:
        def sbuf(name, shape, dt):
            return es.enter_context(nc.sbuf_tensor(name, shape, dt))

        xnT = sbuf("xnT", [128, KT * HALF], BF16)
        retrT = sbuf("retrT", [128, H * HALF], BF16)
        m_bf = sbuf("m_bf", [128, H * 128], BF16)
        zrep = sbuf("zrep", [128, H * 128], BF16)
        b_rep = sbuf("b_rep", [128, D], BF16)
        ident_bf = sbuf("ident_bf", [128, 128], BF16)
        ones_sq = sbuf("ones_sq", [128, 128], BF16)
        onesf = sbuf("onesf", [128, 128], F32)
        z_g = [sbuf(f"z_g{g}", [128, 1], F32) for g in range(H)]
        bq_g = [sbuf(f"bq_g{g}", [128, 1], F32) for g in range(H)]
        c_eps5 = sbuf("c_eps5", [128, 1], F32)
        c_eps6 = sbuf("c_eps6", [128, 1], F32)
        c_invd = sbuf("c_invd", [128, 1], F32)
        c_neg1 = sbuf("c_neg1", [128, 1], F32)
        c_one = sbuf("c_one", [128, 1], F32)
        c_i128 = sbuf("c_i128", [128, 1], F32)
        c_junk = sbuf("c_junk", [128, 1], F32)
        c_nsum = sbuf("c_nsum", [128, 1], F32)
        c_nm = sbuf("c_nm", [128, 1], F32)
        c_ssq = sbuf("c_ssq", [128, 1], F32)
        c_nm2n = sbuf("c_nm2n", [128, 1], F32)
        c_var = sbuf("c_var", [128, 1], F32)
        c_sig = sbuf("c_sig", [128, 1], F32)
        c_inv = sbuf("c_inv", [128, 1], F32)

        ps_q = [es.enter_context(nc.psum_tensor(f"ps_q{i}", [128, 512], F32)) for i in range(2)]
        ps_n = [es.enter_context(nc.psum_tensor(f"ps_n{i}", [128, 512], F32)) for i in range(2)]
        ps_b = es.enter_context(nc.psum_tensor("ps_b", [128, 512], F32))
        ps_d = es.enter_context(nc.psum_tensor("ps_d", [128, 512], F32))
        ps_tr = [es.enter_context(nc.psum_tensor(f"ps_tr{i}", [128, 128], F32)) for i in range(2)]

        sem_names = ["sxt0", "sxt1", "swq0", "swq1", "sw80", "sw81", "ssc0", "ssc1", "ssc2", "ssc3",
                     "sxr0", "sxr1", "sout0", "sout1", "sacc0", "sacc1", "sacc2", "sacc3", "swos",
                     "swot0", "swot1", "scst", "swqs", "swqc", "swoc0", "swoc1", "e_two",
                     "e_g", "e_ln", "e_lnr", "e_vtr", "e_ttr",
                     "e_tq", "e_sg", "e_vsig", "e_tn", "e_vr", "e_tb", "e_vretr",
                     "e_dqc", "e_dqp", "e_vwoc", "e_ev", "e_tf", "ph"]
        S = {n: es.enter_context(nc.semaphore(n)) for n in sem_names}
        C = {n: 0 for n in sem_names}

        n_xt = n_wq = n_w8 = n_sc = n_xr = n_out = n_acc = 0

        block = es.enter_context(nc.Block())

        def emit(ops):
            for eng in ENGS:
                lst = ops.get(eng)
                if not lst:
                    continue
                getattr(block, eng)(lambda h, lst=lst: [f(h) for f in lst])

        def mkops():
            return {e: [] for e in ENGS}

        def barrier():
            ops = mkops()
            C["ph"] += 5
            tgt = C["ph"]
            pre = {
                "tensor": [("e_vretr", C["e_vretr"]), ("e_ev", C["e_ev"]),
                           ("e_vtr", C["e_vtr"]), ("e_vsig", C["e_vsig"])],
                "gpsimd": [("sout0", 16 * ((n_out + 1) // 2)), ("sout1", 16 * (n_out // 2)),
                           ("sacc0", C["sacc0"]), ("sacc1", C["sacc1"]),
                           ("sacc2", C["sacc2"]), ("sacc3", C["sacc3"]), ("swos", C["swos"])],
                "vector": [("e_tf", C["e_tf"])],
                "scalar": [],
                "sync": [],
            }
            for eng in ENGS:
                def f(h, t=tgt, waits=tuple(pre[eng])):
                    for sname, val in waits:
                        if val:
                            h.wait_ge(S[sname], val)
                    h.nop().then_inc(S["ph"], 1)
                    h.wait_ge(S["ph"], t)
                ops[eng].append(f)
            emit(ops)

        sg_of, tn_of, vr_of, retr_of = {}, {}, {}, {}

        if stop == 'bar':
            barrier()
            return nc
        es_pro = ExitStack()
        bcol = [es_pro.enter_context(nc.sbuf_tensor(f"bcol{j}", [128, 1], F32))
                for j in range(KT)]
        rep_f = es_pro.enter_context(nc.sbuf_tensor("rep_f", [128, 128], F32))
        rep_b = es_pro.enter_context(nc.sbuf_tensor("rep_b", [128, 128], BF16))
        ops = mkops()

        def g_pro(g):
            g.memset(ones_sq[:, :], 1.0)
            g.memset(onesf[:, :], 1.0)
            g.memset(c_eps5[:, :], 1e-5)
            g.memset(c_eps6[:, :], 1e-6)
            g.memset(c_invd[:, :], 1.0 / D)
            g.memset(c_neg1[:, :], -1.0)
            g.memset(c_one[:, :], 1.0)
            g.memset(c_i128[:, :], 1.0 / 128.0)
            g.memset(c_junk[:, :], 0.0).then_inc(S["e_g"], 1)
            g.dma_start(out=m_bf[:, :].rearrange("p (g e) -> p g e", g=H),
                        in_=m_in.rearrange("g d e -> d g e")).then_inc(S["scst"], 16)
            g.dma_start(out=ident_bf[:, :], in_=ident[:, :]).then_inc(S["scst"], 16)
            for j in range(KT):
                g.dma_start(out=bcol[j][:, :],
                            in_=bff.rearrange("o f -> f o")[j * 128:(j + 1) * 128, :],
                            allow_slow_non_contiguous=True).then_inc(S["scst"], 16)
            for gg in range(H):
                g.dma_start(out=z_g[gg][:, :], in_=z_in.rearrange("g d -> d g")[:, gg:gg + 1],
                            allow_slow_non_contiguous=True).then_inc(S["scst"], 16)
                g.dma_start(out=bq_g[gg][:, :], in_=bq2.rearrange("g d -> d g")[:, gg:gg + 1],
                            allow_slow_non_contiguous=True).then_inc(S["scst"], 16)
        ops["gpsimd"].append(g_pro)
        C["e_g"] += 1
        C["scst"] += 16 * (2 + KT + 2 * H)
        cst_done = C["scst"]
        if stop == 'pro_g':
            emit(ops)
            es_pro.close()
            return nc

        def s_pro(s, cst_done=cst_done):
            s.wait_ge(S["e_g"], 1)
            s.wait_ge(S["scst"], cst_done)
            ins = None
            for gg in range(H):
                ins = s.activation(zrep[:, gg * 128:(gg + 1) * 128], onesf[:, :], AF.Copy,
                                   scale=z_g[gg][:, 0:1])
            ins.then_inc(S["e_sg"], 1)
        ops["scalar"].append(s_pro)
        C["e_sg"] += 1
        if stop == 'pro_s':
            emit(ops)
            es_pro.close()
            return nc

        def v_pro(v, cst_done=cst_done):
            v.wait_ge(S["e_g"], 1)
            v.wait_ge(S["scst"], cst_done)
            ins = None
            for j in range(KT):
                ins = v.tensor_scalar(bcol[j][:, :], bcol[j][:, :], c_i128[:, 0:1], None, ALU.mult)
            ins.then_inc(S["e_ln"], 1)
        ops["vector"].append(v_pro)
        C["e_ln"] += 1
        if stop == 'pro_v':
            emit(ops)
            es_pro.close()
            return nc

        for j in range(KT):
            def s_bj(s, j=j, v_need=C["e_vsig"] + 0, ln_need=C["e_ln"]):
                if j == 0:
                    s.wait_ge(S["e_ln"], ln_need)
                else:
                    s.wait_ge(S["e_vsig"], v_need)
                s.activation(rep_f[:, :], onesf[:, :], AF.Copy,
                             scale=bcol[j][:, 0:1]).then_inc(S["e_sg"], 1)
            ops["scalar"].append(s_bj)
            C["e_sg"] += 1

            def v_bj(v, j=j, s_need=C["e_sg"], t_need=C["e_tb"]):
                v.wait_ge(S["e_sg"], s_need)
                if j >= 1:
                    v.wait_ge(S["e_tb"], t_need)
                v.tensor_copy(rep_b[:, :], rep_f[:, :]).then_inc(S["e_vsig"], 1)
            ops["vector"].append(v_bj)
            C["e_vsig"] += 1

            def t_bj(t, j=j, v_need=C["e_vsig"], e_need=C["e_vretr"]):
                if j == 0:
                    t.wait_ge(S["e_g"], 1)
                t.wait_ge(S["e_vsig"], v_need)
                if j >= 1:
                    t.wait_ge(S["e_vretr"], e_need)
                t.matmul(ps_b[:, 0:128], rep_b[:, :], ident_bf[:, :],
                         start=True, stop=True).then_inc(S["e_tb"], 1)
            ops["tensor"].append(t_bj)
            C["e_tb"] += 1

            def v_bj2(v, j=j, t_need=C["e_tb"]):
                v.wait_ge(S["e_tb"], t_need)
                v.tensor_copy(b_rep[:, j * 128:(j + 1) * 128], ps_b[:, 0:128]).then_inc(S["e_vretr"], 1)
            ops["vector"].append(v_bj2)
            C["e_vretr"] += 1

        emit(ops)
        es_pro.close()
        barrier()

        halves = [] if stop == 'pro' else ([0] if stop in ('ln0', 'a0', 'b0') else [0, 1])
        for half in halves:
            with ExitStack() as esA:
                def sbA(name, shape, dt):
                    return esA.enter_context(nc.sbuf_tensor(f"{name}_h{half}", shape, dt))
                x_t = [sbA(f"x_t{i}", [128, D], F32) for i in range(2)]
                xsq = sbA("xsq", [128, D], BF16)
                xn = [sbA(f"xn{i}", [128, D], BF16) for i in range(2)]
                wq_b = [sbA(f"wq_b{i}", [128, KT * 128], BF16) for i in range(2)]
                esb = sbA("esb", [128, 512], F32)
                rsb = sbA("rsb", [128, 512], F32)
                sig_bf = [sbA(f"sig_bf{i}", [128, 512], BF16) for i in range(2)]
                dnf = sbA("dnf", [128, 512], F32)
                if half == 0:
                    wo_stage = sbA("wo_stage", [128, D], BF16)
                    trg = [sbA(f"trg{i}", [128, D], BF16) for i in range(2)]
                    wo_out = [sbA(f"wo_out{i}", [128, 1024], BF16) for i in range(2)]
                rcp_f = sbA("rcp_f", [128, 512], F32)
                denb = sbA("denb", [128, 512], BF16)

                ops = mkops()
                ln_base = C["e_ln"]
                lnr_base = C["e_lnr"]
                vtr_base = C["e_vtr"]
                ttr_base = C["e_ttr"]
                swoc_base = {0: C["swoc0"], 1: C["swoc1"]}
                wo_two_base = C["e_two"]
                wo_vwoc_base = C["e_vwoc"]
                swos_base = C["swos"]
                wo_ps = [ps_n[0], ps_n[1], ps_d, ps_b]

                for tloc in range(NTILE):
                    T = half * NTILE + tloc

                    def f_x(sy, T=T, tloc=tloc, need=ln_base + tloc - 1):
                        if tloc >= 2:
                            sy.wait_ge(S["e_ln"], need)
                        sy.dma_start(out=x_t[tloc % 2][:, :],
                                     in_=x_in[T * 128:(T + 1) * 128, :]).then_inc(S[f"sxt{T % 2}"], 16)
                    ops["sync"].append(f_x)
                    n_xt += 1

                for tloc in range(NTILE):
                    Tg = half * NTILE + tloc
                    xt_sem = f"sxt{Tg % 2}"
                    xt_need = 16 * (Tg // 2 + 1)

                    def s_sq(s, tloc=tloc, xt_sem=xt_sem, xt_need=xt_need, lnr_need=lnr_base + tloc):
                        s.wait_ge(S[xt_sem], xt_need)
                        if tloc >= 1:
                            s.wait_ge(S["e_lnr"], lnr_need)
                        s.activation(xsq[:, :], x_t[tloc % 2][:, :],
                                     AF.Square).then_inc(S["e_sg"], 1)
                    ops["scalar"].append(s_sq)
                    C["e_sg"] += 1
                    sq_done = C["e_sg"]

                    def v_st(v, tloc=tloc, xt_sem=xt_sem, xt_need=xt_need, sq_done=sq_done):
                        v.wait_ge(S[xt_sem], xt_need)
                        xt = x_t[tloc % 2]
                        v.tensor_reduce(c_nsum[:, :], xt[:, :], AXX, ALU.add, negate=True)
                        v.tensor_scalar(c_nm[:, :], c_nsum[:, :], c_invd[:, 0:1], None, ALU.mult)
                        v.wait_ge(S["e_sg"], sq_done)
                        v.tensor_reduce(c_ssq[:, :], xsq[:, :], AXX, ALU.add).then_inc(S["e_lnr"], 1)
                        v.tensor_scalar(c_nm2n[:, :], c_nm[:, :], c_nm[:, 0:1], c_neg1[:, 0:1],
                                        ALU.mult, ALU.mult)
                        v.tensor_scalar(c_ssq[:, :], c_ssq[:, :], c_invd[:, 0:1], None, ALU.mult)
                        v.tensor_copy(c_junk[:, :], c_one[:, :])
                        v.tensor_scalar(c_var[:, :], c_ssq[:, :], c_nm2n[:, 0:1],
                                        None, ALU.add).then_inc(S["e_vr"], 1)
                    ops["vector"].append(v_st)
                    C["e_lnr"] += 1
                    C["e_vr"] += 1
                    var_done = C["e_vr"]

                    def s_rt(s, var_done=var_done):
                        s.wait_ge(S["e_vr"], var_done)
                        s.activation(c_sig[:, :], c_var[:, :], AF.Sqrt,
                                     bias=c_eps5[:, 0:1]).then_inc(S["e_sg"], 1)
                    ops["scalar"].append(s_rt)
                    C["e_sg"] += 1
                    sig_done = C["e_sg"]
                    if half == 0:
                        def s_woev(s, tloc=tloc):
                            for jj in range(4 * tloc, 4 * tloc + 4):
                                emit_wo_evac(s, jj)
                        ops["scalar"].append(s_woev)
                        C["e_vwoc"] += 8

                    def v_xn(v, tloc=tloc, sig_done=sig_done,
                             ttr_need=ttr_base + 32 * (tloc - 1)):
                        v.wait_ge(S["e_sg"], sig_done)
                        v.reciprocal(c_inv[:, :], c_sig[:, :])
                        v.tensor_copy(c_junk[:, :], c_one[:, :])
                        if tloc >= 2:
                            v.wait_ge(S["e_ttr"], ttr_need)
                        v.tensor_scalar(xn[tloc % 2][:, :], x_t[tloc % 2][:, :],
                                        c_nm[:, 0:1], c_inv[:, 0:1],
                                        ALU.add, ALU.mult).then_inc(S["e_ln"], 1)
                    ops["vector"].append(v_xn)
                    C["e_ln"] += 1
                    xn_done = C["e_ln"]

                    def t_tr(t, tloc=tloc, xn_done=xn_done, vtr_b=vtr_base + 32 * tloc,
                             half=half):
                        t.wait_ge(S["e_ln"], xn_done)
                        for k in range(KT):
                            gk = vtr_b - vtr_base + k
                            if gk >= 2:
                                t.wait_ge(S["e_vtr"], vtr_b + k - 1)
                            t.matmul(ps_tr[k % 2][:, :], xn[tloc % 2][:, k * 128:(k + 1) * 128],
                                     ident_bf[:, :], start=True,
                                     stop=True).then_inc(S["e_ttr"], 1)
                        if half == 0:
                            for jj in range(4 * tloc, 4 * tloc + 4):
                                emit_wo_mm(t, jj)
                    ops["tensor"].append(t_tr)
                    C["e_ttr"] += KT

                    def v_tr(v, tloc=tloc, ttr_b=ttr_base + 32 * tloc):
                        for k in range(KT):
                            v.wait_ge(S["e_ttr"], ttr_b + k + 1)
                            v.tensor_copy(
                                xnT[:, k * HALF + tloc * 128: k * HALF + (tloc + 1) * 128],
                                ps_tr[k % 2][:, :]).then_inc(S["e_vtr"], 1)
                    ops["vector"].append(v_tr)
                    C["e_vtr"] += KT

                all_tr = C["e_vtr"]
                base_tq = C["e_tq"]
                do_attn = stop not in ('ln0',)
                a_stage = 5
                if stop.startswith('a1h'):
                    a_stage = int(stop[4:])

                mlist = list(range(H)) if do_attn else []
                if stop.startswith('a1h'):
                    mlist = [0]
                wq_cnt_of = {}

                def emit_wq(m):
                    def f_wq(sy, m=m, tq_need=base_tq + 2 * (m - 1)):
                        if m >= 2:
                            sy.wait_ge(S["e_tq"], tq_need)
                        else:
                            sy.wait_ge(S["swqs"], 128)
                        sy.dma_start(
                            out=wq_b[m % 2][:, :].rearrange("p (k q) -> p k q", k=KT),
                            in_=wqT_s.rearrange("k p q -> p k q")[:, :, m * 128:(m + 1) * 128],
                        ).then_inc(S[f"swq{m % 2}"], 16)
                    ops["sync"].append(f_wq)

                def emit_wo_casts(j):
                    g_idx, q = j // 4, j % 4
                    def f_c(g, j=j, g_idx=g_idx, q=q, two_need=wo_two_base + 2 * (j - 1)):
                        if j >= 2:
                            g.wait_ge(S["e_two"], two_need)
                        for r in range(4):
                            g.dma_start(
                                out=trg[j % 2][:, r * 1024:(r + 1) * 1024],
                                in_=woTb[r, g_idx * 128:(g_idx + 1) * 128,
                                         q * 1024:(q + 1) * 1024],
                            ).then_inc(S[f"swoc{j % 2}"], 16)
                    ops["gpsimd"].append(f_c)
                    C[f"swoc{j % 2}"] += 64

                def emit_wo_store(j):
                    def f_s(g, j=j, g_idx=j // 4, q=j % 4,
                            ev_need=wo_vwoc_base + 2 * j + 2):
                        g.wait_ge(S["e_vwoc"], ev_need)
                        g.dma_start(out=woT_s[g_idx][:, q * 1024:(q + 1) * 1024],
                                    in_=wo_out[j % 2][:, :]).then_inc(S["swos"], 16)
                    ops["gpsimd"].append(f_s)
                    C["swos"] += 16

                def emit_wo_mm(t, j):
                    cast_need = swoc_base[j % 2] + 64 * (j // 2 + 1)
                    for oc in range(2):
                        sidx = 2 * j + oc
                        if sidx - 3 > 0:
                            t.wait_ge(S["e_vwoc"], wo_vwoc_base + sidx - 3)
                        t.wait_ge(S[f"swoc{j % 2}"], cast_need)
                        ps = wo_ps[sidx % 4]
                        ins = None
                        for r in range(4):
                            ins = t.matmul(ps[:, 0:512], ident_bf[:, :],
                                           trg[j % 2][:, r * 1024 + oc * 512:
                                                      r * 1024 + oc * 512 + 512],
                                           start=(r == 0), stop=(r == 3))
                        ins.then_inc(S["e_two"], 1)

                def emit_wo_evac(s, j):
                    for oc in range(2):
                        sidx = 2 * j + oc
                        s.wait_ge(S["e_two"], wo_two_base + sidx + 1)
                        if j >= 2 and oc == 0:
                            s.wait_ge(S["swos"], swos_base + 16 * (j - 1))
                        s.copy(wo_out[j % 2][:, oc * 512:(oc + 1) * 512],
                               wo_ps[sidx % 4][:, 0:512]).then_inc(S["e_vwoc"], 1)

                if mlist:
                    if half == 0:
                        def f_strips(g):
                            for s in range(8):
                                if s >= 1:
                                    g.wait_ge(S["swqs"], 16 * s)
                                g.dma_start(
                                    out=wo_stage[:, :].rearrange("p (k q) -> p k q", k=4),
                                    in_=wqT.rearrange("(s k p) q -> s p k q",
                                                      s=8, p=128)[s],
                                ).then_inc(S["swqc"], 16)
                                g.wait_ge(S["swqc"], 16 * (s + 1))
                                g.dma_start(
                                    out=wqT_s[4 * s:4 * s + 4].rearrange("k p q -> p k q"),
                                    in_=wo_stage[:, :].rearrange("p (k q) -> p k q", k=4),
                                ).then_inc(S["swqs"], 16)
                        ops["gpsimd"].append(f_strips)
                        C["swqc"] += 128
                        C["swqs"] += 128
                        for j in range(32):
                            emit_wo_casts(j)
                            if j >= 1:
                                emit_wo_store(j - 1)
                        emit_wo_store(31)
                    for m in mlist:
                        emit_wq(m)
                    for m in mlist:
                        n_wq += 1
                        wq_cnt_of[m] = (f"swq{m % 2}", 16 * ((half * H + m) // 2 + 1))

                for m in mlist:
                    for n in (range(NSEQ) if not stop.startswith('a1h') else [0]):
                        s_idx = C["e_tq"]

                        def t_q(t, m=m, n=n, s_idx=s_idx, all_tr=all_tr, base_tq=base_tq,
                                wq_need=wq_cnt_of[m], half=half, wo_done=C["e_vwoc"]):
                            if m == 0 and n == 0:
                                t.wait_ge(S["e_vtr"], all_tr)
                                if half == 0:
                                    t.wait_ge(S["e_vwoc"], wo_done)
                            t.wait_ge(S[wq_need[0]], wq_need[1])
                            if s_idx - 2 >= base_tq:
                                t.wait_ge(S["e_sg"], sg_of[s_idx - 2])
                            ins = None
                            for k in range(KT):
                                ins = t.matmul(ps_q[n][:, :], wq_b[m % 2][:, k * 128:(k + 1) * 128],
                                               xnT[:, k * HALF + n * 512: k * HALF + (n + 1) * 512],
                                               start=(k == 0), stop=(k == KT - 1))
                            ins.then_inc(S["e_tq"], 1)
                        ops["tensor"].append(t_q)
                        C["e_tq"] += 1
                        if a_stage < 2:
                            continue
                        tq_done = C["e_tq"]

                        def s_sig(s, m=m, n=n, tq_done=tq_done):
                            s.wait_ge(S["e_tq"], tq_done)
                            s.activation(esb[:, :], ps_q[n][:, :], AF.Exp, bias=bq_g[m][:, 0:1])
                            s.activation(rsb[:, :], ps_q[n][:, :], AF.Relu,
                                         bias=bq_g[m][:, 0:1]).then_inc(S["e_sg"], 1)
                        ops["scalar"].append(s_sig)
                        C["e_sg"] += 1
                        sg_of[s_idx] = C["e_sg"]
                        if a_stage < 3:
                            continue

                        def v_sig(v, n=n, s_idx=s_idx, base_tq=base_tq, ready=C["e_sg"]):
                            v.wait_ge(S["e_sg"], ready)
                            if s_idx - 2 >= base_tq:
                                v.wait_ge(S["e_tn"], tn_of[s_idx - 2])
                            v.tensor_scalar(esb[:, :], esb[:, :], c_one[:, 0:1], None, ALU.min)
                            v.tensor_copy(c_junk[:, :], c_one[:, :])
                            v.tensor_tensor(sig_bf[n][:, :], rsb[:, :], esb[:, :],
                                            ALU.add).then_inc(S["e_vsig"], 1)
                        ops["vector"].append(v_sig)
                        C["e_vsig"] += 1
                        if a_stage < 4:
                            continue

                        def t_nd(t, m=m, n=n, s_idx=s_idx, base_tq=base_tq,
                                 vsig_done=C["e_vsig"]):
                            t.wait_ge(S["e_vsig"], vsig_done)
                            if s_idx - 1 >= base_tq:
                                t.wait_ge(S["e_vretr"], retr_of[s_idx - 1])
                            t.matmul(ps_n[n][:, :], m_bf[:, m * 128:(m + 1) * 128],
                                     sig_bf[n][:, :], start=True, stop=True)
                            t.matmul(ps_d[:, :], zrep[:, m * 128:(m + 1) * 128],
                                     sig_bf[n][:, :], start=True,
                                     stop=True).then_inc(S["e_tn"], 1)
                        ops["tensor"].append(t_nd)
                        C["e_tn"] += 1
                        tn_of[s_idx] = C["e_tn"]
                        if a_stage < 5:
                            continue

                        def v_retr(v, m=m, n=n, nd_done=C["e_tn"]):
                            v.wait_ge(S["e_tn"], nd_done)
                            v.tensor_scalar(dnf[:, :], ps_d[:, :], c_eps6[:, 0:1],
                                            None, ALU.add)
                            v.tensor_copy(c_junk[:, :], c_one[:, :])
                            v.reciprocal(rcp_f[:, :], dnf[:, :])
                            v.tensor_copy(c_junk[:, :], c_one[:, :])
                            v.tensor_copy(denb[:, :], rcp_f[:, :])
                            v.tensor_copy(c_junk[:, :], c_one[:, :])
                            v.tensor_tensor(
                                retrT[:, m * HALF + n * 512: m * HALF + (n + 1) * 512],
                                ps_n[n][:, :], denb[:, :], ALU.mult).then_inc(S["e_vretr"], 1)
                        ops["vector"].append(v_retr)
                        C["e_vretr"] += 1
                        retr_of[s_idx] = C["e_vretr"]

                emit(ops)
            barrier()
            if stop in ('ln0', 'a0') or stop.startswith('a1h'):
                continue

            with ExitStack() as esB:
                def sbB(name, shape, dt):
                    return esB.enter_context(nc.sbuf_tensor(f"{name}_h{half}", shape, dt))
                wdq = [sbB(f"wdq{i}", [128, KT * CW], BF16) for i in range(2)]
                w8b = [sbB(f"w8b{i}", [128, KT * CW], I8) for i in range(2)]
                scb = [sbB(f"scb{i}", [128, 4 * CW], F32) for i in range(4)]
                wobf = [sbB(f"wobf{i}", [128, H * CW], BF16) for i in range(2)]
                xrb = [sbB(f"xrb{i}", [128, NTILE * CW], F32) for i in range(2)]
                ost = [sbB(f"ost{i}", [128, NTILE * CW], F32) for i in range(2)]

                ops = mkops()
                tf_base = C["e_tf"]
                ev_base = C["e_ev"]
                dqc_base = C["e_dqp"] // 8
                out_base = n_out
                xr_base = n_xr
                retr_ready = C["e_vretr"]

                def emit_evacs(v, c, half=half, ev_cnt_base=ev_base, tf_b=tf_base,
                               xr_b=xr_base, out_b=out_base):
                    cg = half * NCH + c
                    for tt in range(NTILE):
                        v.wait_ge(S["e_tf"], tf_b + c * NTILE + tt + 1)
                        if tt == 0:
                            v.wait_ge(S[f"sxr{c % 2}"], 16 * (cg // 2 + 1))
                            if c >= 2:
                                v.wait_ge(S[f"sout{c % 2}"], 16 * (cg // 2))
                        v.tensor_tensor(ost[c % 2][:, tt * CW:(tt + 1) * CW],
                                        ps_q[tt % 2][:, 0:CW],
                                        xrb[c % 2][:, tt * CW:(tt + 1) * CW],
                                        ALU.add).then_inc(S["e_ev"], 1)

                for c in range(NCH):
                    cc0 = c * CW

                    def sy_w8(sy, c=c, cc0=cc0, dq_need=8 * (dqc_base + c - 1)):
                        if c >= 2:
                            sy.wait_ge(S["e_dqp"], dq_need)
                        sy.dma_start(
                            out=w8b[c % 2][:, :].rearrange("p (k j) -> p k j", k=KT),
                            in_=w8.rearrange("(k p) f -> p k f", p=128)[:, :, cc0:cc0 + CW],
                        ).then_inc(S[f"sw8{c % 2}"], 16)
                    ops["sync"].append(sy_w8)
                    n_w8 += 1

                    for kg in range(8):
                        def sy_sc(sy, c=c, cc0=cc0, kg=kg, pidx=n_sc):
                            if pidx >= 4:
                                sy.wait_ge(S["e_dqp"], pidx - 3)
                            sy.dma_start(
                                out=scb[pidx % 4][:, :].rearrange("p (kk j) -> p kk j", kk=4),
                                in_=wsc.rearrange("(kg kk p) f -> kg p kk f",
                                                  kg=8, p=128)[kg][:, :, cc0:cc0 + CW],
                            ).then_inc(S[f"ssc{pidx % 4}"], 16)
                        ops["sync"].append(sy_sc)
                        n_sc += 1

                    def sy_xr(sy, c=c, cc0=cc0, half=half, ev_need=ev_base + 8 * (c - 1)):
                        if c >= 2:
                            sy.wait_ge(S["e_ev"], ev_need)
                        sy.dma_start(
                            out=xrb[c % 2][:, :].rearrange("p (tt j) -> p tt j", tt=NTILE),
                            in_=x_in.rearrange("(hh tt p) f -> hh tt p f", hh=2, p=128)[half]
                                    .rearrange("tt p f -> p tt f")[:, :, cc0:cc0 + CW],
                        ).then_inc(S[f"sxr{c % 2}"], 16)
                    ops["sync"].append(sy_xr)
                    n_xr += 1

                    def sy_wot(sy, c=c, cc0=cc0, half=half, tf_need=tf_base + 8 * (c - 1)):
                        if half == 0 and c == 0:
                            sy.wait_ge(S["swos"], 16 * 32)
                        if c >= 2:
                            sy.wait_ge(S["e_tf"], tf_need)
                        sy.dma_start(
                            out=wobf[c % 2][:, :].rearrange("p (g j) -> p g j", g=H),
                            in_=woT_s.rearrange("g p f -> p g f")[:, :, cc0:cc0 + CW],
                        ).then_inc(S[f"swot{c % 2}"], 16)
                    if not nowo:
                        ops["gpsimd"].append(sy_wot)

                    if c >= 1:
                        def g_out(g, cprev=c - 1, half=half, ev_need=ev_base + 8 * c,
                                  so=n_out):
                            g.wait_ge(S["e_ev"], ev_need)
                            if so >= 2:
                                g.wait_ge(S[f"sout{so % 2}"], 16 * (so // 2))
                            g.dma_start(
                                out=out.rearrange("(hh tt p) f -> hh tt p f", hh=2, p=128)[half]
                                       .rearrange("tt p f -> p tt f")[:, :, cprev * CW:(cprev + 1) * CW],
                                in_=ost[cprev % 2][:, :].rearrange("p (tt j) -> p tt j", tt=NTILE),
                            ).then_inc(S[f"sout{so % 2}"], 16)
                        ops["gpsimd"].append(g_out)
                        n_out += 1

                    def v_dq(v, c=c, w8_sem=f"sw8{c % 2}", w8_need=16 * ((half * NCH + c) // 2 + 1),
                             piece0=n_sc - 8, tf_need=tf_base + 8 * (c - 1)):
                        v.wait_ge(S[w8_sem], w8_need)
                        if c >= 2:
                            v.wait_ge(S["e_tf"], tf_need)
                        for k in range(KT):
                            kg = k // 4
                            if k % 4 == 0:
                                pp = piece0 + kg
                                v.wait_ge(S[f"ssc{pp % 4}"], 16 * (pp // 4 + 1))
                            pidx = piece0 + kg
                            ins = v.tensor_tensor(wdq[c % 2][:, k * CW:(k + 1) * CW],
                                                  w8b[c % 2][:, k * CW:(k + 1) * CW],
                                                  scb[pidx % 4][:, (k % 4) * CW:(k % 4 + 1) * CW],
                                                  ALU.mult)
                            if k % 4 == 3:
                                ins.then_inc(S["e_dqp"], 1)
                    ops["vector"].append(v_dq)
                    C["e_dqp"] += 8

                    if c >= 1:
                        def v_ev(v, cprev=c - 1):
                            emit_evacs(v, cprev)
                        ops["vector"].append(v_ev)
                        C["e_ev"] += NTILE

                    def t_ch(t, c=c, dq_need=8 * (dqc_base + c + 1),
                             wo_sem=f"swot{c % 2}", wo_need=16 * ((half * NCH + c) // 2 + 1),
                             ev_b=ev_base, retr_ready=retr_ready):
                        if c == 0:
                            t.wait_ge(S["e_vretr"], retr_ready)
                        t.wait_ge(S["e_dqp"], dq_need)
                        if not nowo:
                            t.wait_ge(S[wo_sem], wo_need)
                        for tt in range(NTILE):
                            gi = c * NTILE + tt
                            if gi >= 2:
                                t.wait_ge(S["e_ev"], ev_b + gi - 1)
                            psq = ps_q[tt % 2]
                            for k in range(KT):
                                t.matmul(psq[:, 0:CW],
                                         xnT[:, k * HALF + tt * 128: k * HALF + (tt + 1) * 128],
                                         wdq[c % 2][:, k * CW:(k + 1) * CW],
                                         start=(k == 0), stop=False)
                            if not (nowo or nowomm):
                                for g in range(H):
                                    t.matmul(psq[:, 0:CW],
                                             retrT[:, g * HALF + tt * 128: g * HALF + (tt + 1) * 128],
                                             wobf[c % 2][:, g * CW:(g + 1) * CW],
                                             start=False, stop=False)
                            t.matmul(psq[:, 0:CW], ones_sq[:, :], b_rep[:, c * CW:(c + 1) * CW],
                                     start=False, stop=True).then_inc(S["e_tf"], 1)
                    ops["tensor"].append(t_ch)
                    C["e_tf"] += NTILE

                def v_ev_last(v, cprev=NCH - 1):
                    emit_evacs(v, cprev)
                ops["vector"].append(v_ev_last)
                C["e_ev"] += NTILE

                def g_out_last(g, cprev=NCH - 1, half=half, ev_need=ev_base + 8 * NCH,
                               so=n_out):
                    g.wait_ge(S["e_ev"], ev_need)
                    if so >= 2:
                        g.wait_ge(S[f"sout{so % 2}"], 16 * (so // 2))
                    g.dma_start(
                        out=out.rearrange("(hh tt p) f -> hh tt p f", hh=2, p=128)[half]
                               .rearrange("tt p f -> p tt f")[:, :, cprev * CW:(cprev + 1) * CW],
                        in_=ost[cprev % 2][:, :].rearrange("p (tt j) -> p tt j", tt=NTILE),
                    ).then_inc(S[f"sout{so % 2}"], 16)
                ops["gpsimd"].append(g_out_last)
                n_out += 1

                emit(ops)
            barrier()

        if n_out:
            ops = mkops()
            t0v = 16 * ((n_out + 1) // 2)
            t1v = 16 * (n_out // 2)
            def g_fin(g, t0v=t0v, t1v=t1v):
                g.wait_ge(S["sout0"], t0v)
                g.wait_ge(S["sout1"], t1v)
            ops["gpsimd"].append(g_fin)
            emit(ops)

    return nc


_NC_CACHE = {}


def kernel(L0_attn_hidden_states, L0_attn_Wq, L0_attn_bq, L0_attn_Wk, L0_attn_bk,
           L0_attn_Wv, L0_attn_bv, L0_attn_Wo, L0_attn_M, L0_attn_z,
           L0_ffn_w, L0_ffn_w_scale, L0_ffn_b):
    x = np.ascontiguousarray(np.asarray(L0_attn_hidden_states, dtype=np.float32))
    Wq = np.asarray(L0_attn_Wq, dtype=np.float32)
    bq = np.asarray(L0_attn_bq, dtype=np.float32)
    Wo = np.asarray(L0_attn_Wo, dtype=np.float32)
    M = np.ascontiguousarray(np.asarray(L0_attn_M, dtype=np.float32))
    z = np.ascontiguousarray(np.asarray(L0_attn_z, dtype=np.float32))
    w8 = np.ascontiguousarray(np.asarray(L0_ffn_w, dtype=np.int8))
    wsc = np.ascontiguousarray(np.asarray(L0_ffn_w_scale, dtype=np.float32))
    bff = np.ascontiguousarray(np.asarray(L0_ffn_b, dtype=np.float32).reshape(1, D))

    Wq_sel = Wq.reshape(8, 4, 128, D)[:, 0].reshape(8 * 128, D)
    wqT = np.ascontiguousarray(Wq_sel.T)
    bq2 = np.ascontiguousarray(bq.reshape(8, 4, 128)[:, 0])
    woTb = np.ascontiguousarray(
        Wo.T.reshape(8, 4, 128, D).transpose(1, 0, 2, 3).reshape(4, 1024, D))
    ident = np.eye(128, dtype=np.float32)

    if "nc" not in _NC_CACHE:
        _NC_CACHE["nc"] = build_program()
    nc = _NC_CACHE["nc"]

    in_maps = []
    for i in range(8):
        in_maps.append(dict(x_in=np.ascontiguousarray(x[i]), wqT=wqT, bq2=bq2,
                            m_in=M, z_in=z, woTb=woTb, w8=w8, wsc=wsc, bff=bff,
                            ident=ident))
    res = run_bass_kernel_spmd(nc, in_maps, core_ids=list(range(8)))
    return np.stack([r["out"] for r in res.results], axis=0)


# version 12
# speedup vs baseline: 1.3303x; 1.1976x over previous
import numpy as np
from contextlib import ExitStack

import concourse.bass as bass
import concourse.mybir as mybir
from concourse.bass_utils import run_bass_kernel_spmd

F32 = mybir.dt.float32
BF16 = mybir.dt.bfloat16
I8 = mybir.dt.int8
AF = mybir.ActivationFunctionType
ALU = mybir.AluOpType
AXX = mybir.AxisListType.X

D = 4096
KT = 32
H = 8
CW = 256
NCH = D // CW
TOK = 2048
HALF = TOK // 2
NTILE = HALF // 128
NSEQ = HALF // 512

ENGS = ("sync", "gpsimd", "scalar", "vector", "tensor")


def build_program(stop='full', nowo=False, nowomm=False):
    nc = bass.Bass("TRN2", target_bir_lowering=False)

    x_in = nc.dram_tensor("x_in", [TOK, D], F32, kind="ExternalInput")
    wqT = nc.dram_tensor("wqT", [D, H * 128], F32, kind="ExternalInput")
    bq2 = nc.dram_tensor("bq2", [H, 128], F32, kind="ExternalInput")
    m_in = nc.dram_tensor("m_in", [H, 128, 128], F32, kind="ExternalInput")
    z_in = nc.dram_tensor("z_in", [H, 128], F32, kind="ExternalInput")
    woTb = nc.dram_tensor("woTb", [4, H * 128, D], F32, kind="ExternalInput")
    w8 = nc.dram_tensor("w8", [D, D], I8, kind="ExternalInput")
    wsc = nc.dram_tensor("wsc", [D, D], F32, kind="ExternalInput")
    bff = nc.dram_tensor("bff", [1, D], F32, kind="ExternalInput")
    ident = nc.dram_tensor("ident", [128, 128], F32, kind="ExternalInput")
    out = nc.dram_tensor("out", [TOK, D], F32, kind="ExternalOutput")
    woT_s = nc.dram_tensor("woT_s", [H, 128, D], BF16, kind="Internal")
    wqT_s = nc.dram_tensor("wqT_s", [KT, 128, H * 128], BF16, kind="Internal")

    with ExitStack() as es:
        def sbuf(name, shape, dt):
            return es.enter_context(nc.sbuf_tensor(name, shape, dt))

        xnT = sbuf("xnT", [128, KT * HALF], BF16)
        retrT = sbuf("retrT", [128, H * HALF], BF16)
        m_bf = sbuf("m_bf", [128, H * 128], BF16)
        zrep = sbuf("zrep", [128, H * 128], BF16)
        b_rep = sbuf("b_rep", [128, D], BF16)
        ident_bf = sbuf("ident_bf", [128, 128], BF16)
        ones_sq = sbuf("ones_sq", [128, 128], BF16)
        onesf = sbuf("onesf", [128, 128], F32)
        z_g = [sbuf(f"z_g{g}", [128, 1], F32) for g in range(H)]
        bq_g = [sbuf(f"bq_g{g}", [128, 1], F32) for g in range(H)]
        c_eps5 = sbuf("c_eps5", [128, 1], F32)
        c_eps6 = sbuf("c_eps6", [128, 1], F32)
        c_invd = sbuf("c_invd", [128, 1], F32)
        c_neg1 = sbuf("c_neg1", [128, 1], F32)
        c_one = sbuf("c_one", [128, 1], F32)
        c_i128 = sbuf("c_i128", [128, 1], F32)
        c_junk = sbuf("c_junk", [128, 1], F32)
        c_nsum = sbuf("c_nsum", [128, 1], F32)
        c_nm = sbuf("c_nm", [128, 1], F32)
        c_ssq = sbuf("c_ssq", [128, 1], F32)
        c_nm2n = sbuf("c_nm2n", [128, 1], F32)
        c_var = sbuf("c_var", [128, 1], F32)
        c_sig = sbuf("c_sig", [128, 1], F32)
        c_inv = sbuf("c_inv", [128, 1], F32)

        ps_q = [es.enter_context(nc.psum_tensor(f"ps_q{i}", [128, 512], F32)) for i in range(2)]
        ps_n = [es.enter_context(nc.psum_tensor(f"ps_n{i}", [128, 512], F32)) for i in range(2)]
        ps_b = es.enter_context(nc.psum_tensor("ps_b", [128, 512], F32))
        ps_d = es.enter_context(nc.psum_tensor("ps_d", [128, 512], F32))
        ps_tr = [es.enter_context(nc.psum_tensor(f"ps_tr{i}", [128, 128], F32)) for i in range(2)]

        sem_names = ["sxt0", "sxt1", "swq0", "swq1", "sw80", "sw81", "ssc0", "ssc1", "ssc2", "ssc3",
                     "sxr0", "sxr1", "sout0", "sout1", "sacc0", "sacc1", "sacc2", "sacc3", "swos",
                     "swot0", "swot1", "scst", "swqs", "swqc",
                     "e_g", "e_ln", "e_lnr", "e_vtr", "e_ttr",
                     "e_tq", "e_sg", "e_vsig", "e_tn", "e_vr", "e_tb", "e_vretr",
                     "e_dqc", "e_dqp", "e_vwoc", "e_ev", "e_tf", "ph"]
        S = {n: es.enter_context(nc.semaphore(n)) for n in sem_names}
        C = {n: 0 for n in sem_names}

        n_xt = n_wq = n_w8 = n_sc = n_xr = n_out = n_acc = 0

        block = es.enter_context(nc.Block())

        def emit(ops):
            for eng in ENGS:
                lst = ops.get(eng)
                if not lst:
                    continue
                getattr(block, eng)(lambda h, lst=lst: [f(h) for f in lst])

        def mkops():
            return {e: [] for e in ENGS}

        def barrier():
            ops = mkops()
            C["ph"] += 5
            tgt = C["ph"]
            pre = {
                "tensor": [("e_vretr", C["e_vretr"]), ("e_ev", C["e_ev"]),
                           ("e_vtr", C["e_vtr"]), ("e_vsig", C["e_vsig"])],
                "gpsimd": [("sout0", 16 * ((n_out + 1) // 2)), ("sout1", 16 * (n_out // 2)),
                           ("sacc0", C["sacc0"]), ("sacc1", C["sacc1"]),
                           ("sacc2", C["sacc2"]), ("sacc3", C["sacc3"]), ("swos", C["swos"])],
                "vector": [("e_tf", C["e_tf"])],
                "scalar": [],
                "sync": [],
            }
            for eng in ENGS:
                def f(h, t=tgt, waits=tuple(pre[eng])):
                    for sname, val in waits:
                        if val:
                            h.wait_ge(S[sname], val)
                    h.nop().then_inc(S["ph"], 1)
                    h.wait_ge(S["ph"], t)
                ops[eng].append(f)
            emit(ops)

        sg_of, tn_of, vr_of, retr_of = {}, {}, {}, {}

        if stop == 'bar':
            barrier()
            return nc
        es_pro = ExitStack()
        bcol = [es_pro.enter_context(nc.sbuf_tensor(f"bcol{j}", [128, 1], F32))
                for j in range(KT)]
        rep_f = es_pro.enter_context(nc.sbuf_tensor("rep_f", [128, 128], F32))
        rep_b = es_pro.enter_context(nc.sbuf_tensor("rep_b", [128, 128], BF16))
        ops = mkops()

        def g_pro(g):
            g.memset(ones_sq[:, :], 1.0)
            g.memset(onesf[:, :], 1.0)
            g.memset(c_eps5[:, :], 1e-5)
            g.memset(c_eps6[:, :], 1e-6)
            g.memset(c_invd[:, :], 1.0 / D)
            g.memset(c_neg1[:, :], -1.0)
            g.memset(c_one[:, :], 1.0)
            g.memset(c_i128[:, :], 1.0 / 128.0)
            g.memset(c_junk[:, :], 0.0).then_inc(S["e_g"], 1)
            g.dma_start(out=m_bf[:, :].rearrange("p (g e) -> p g e", g=H),
                        in_=m_in.rearrange("g d e -> d g e")).then_inc(S["scst"], 16)
            g.dma_start(out=ident_bf[:, :], in_=ident[:, :]).then_inc(S["scst"], 16)
            for j in range(KT):
                g.dma_start(out=bcol[j][:, :],
                            in_=bff.rearrange("o f -> f o")[j * 128:(j + 1) * 128, :],
                            allow_slow_non_contiguous=True).then_inc(S["scst"], 16)
            for gg in range(H):
                g.dma_start(out=z_g[gg][:, :], in_=z_in.rearrange("g d -> d g")[:, gg:gg + 1],
                            allow_slow_non_contiguous=True).then_inc(S["scst"], 16)
                g.dma_start(out=bq_g[gg][:, :], in_=bq2.rearrange("g d -> d g")[:, gg:gg + 1],
                            allow_slow_non_contiguous=True).then_inc(S["scst"], 16)
        ops["gpsimd"].append(g_pro)
        C["e_g"] += 1
        C["scst"] += 16 * (2 + KT + 2 * H)
        cst_done = C["scst"]
        if stop == 'pro_g':
            emit(ops)
            es_pro.close()
            return nc

        def s_pro(s, cst_done=cst_done):
            s.wait_ge(S["e_g"], 1)
            s.wait_ge(S["scst"], cst_done)
            ins = None
            for gg in range(H):
                ins = s.activation(zrep[:, gg * 128:(gg + 1) * 128], onesf[:, :], AF.Copy,
                                   scale=z_g[gg][:, 0:1])
            ins.then_inc(S["e_sg"], 1)
        ops["scalar"].append(s_pro)
        C["e_sg"] += 1
        if stop == 'pro_s':
            emit(ops)
            es_pro.close()
            return nc

        def v_pro(v, cst_done=cst_done):
            v.wait_ge(S["e_g"], 1)
            v.wait_ge(S["scst"], cst_done)
            ins = None
            for j in range(KT):
                ins = v.tensor_scalar(bcol[j][:, :], bcol[j][:, :], c_i128[:, 0:1], None, ALU.mult)
            ins.then_inc(S["e_ln"], 1)
        ops["vector"].append(v_pro)
        C["e_ln"] += 1
        if stop == 'pro_v':
            emit(ops)
            es_pro.close()
            return nc

        for j in range(KT):
            def s_bj(s, j=j, v_need=C["e_vsig"] + 0, ln_need=C["e_ln"]):
                if j == 0:
                    s.wait_ge(S["e_ln"], ln_need)
                else:
                    s.wait_ge(S["e_vsig"], v_need)
                s.activation(rep_f[:, :], onesf[:, :], AF.Copy,
                             scale=bcol[j][:, 0:1]).then_inc(S["e_sg"], 1)
            ops["scalar"].append(s_bj)
            C["e_sg"] += 1

            def v_bj(v, j=j, s_need=C["e_sg"], t_need=C["e_tb"]):
                v.wait_ge(S["e_sg"], s_need)
                if j >= 1:
                    v.wait_ge(S["e_tb"], t_need)
                v.tensor_copy(rep_b[:, :], rep_f[:, :]).then_inc(S["e_vsig"], 1)
            ops["vector"].append(v_bj)
            C["e_vsig"] += 1

            def t_bj(t, j=j, v_need=C["e_vsig"], e_need=C["e_vretr"]):
                if j == 0:
                    t.wait_ge(S["e_g"], 1)
                t.wait_ge(S["e_vsig"], v_need)
                if j >= 1:
                    t.wait_ge(S["e_vretr"], e_need)
                t.matmul(ps_b[:, 0:128], rep_b[:, :], ident_bf[:, :],
                         start=True, stop=True).then_inc(S["e_tb"], 1)
            ops["tensor"].append(t_bj)
            C["e_tb"] += 1

            def v_bj2(v, j=j, t_need=C["e_tb"]):
                v.wait_ge(S["e_tb"], t_need)
                v.tensor_copy(b_rep[:, j * 128:(j + 1) * 128], ps_b[:, 0:128]).then_inc(S["e_vretr"], 1)
            ops["vector"].append(v_bj2)
            C["e_vretr"] += 1

        emit(ops)
        es_pro.close()
        barrier()

        halves = [] if stop == 'pro' else ([0] if stop in ('ln0', 'a0', 'b0') else [0, 1])
        for half in halves:
            with ExitStack() as esA:
                def sbA(name, shape, dt):
                    return esA.enter_context(nc.sbuf_tensor(f"{name}_h{half}", shape, dt))
                x_t = [sbA(f"x_t{i}", [128, D], F32) for i in range(2)]
                xsq = sbA("xsq", [128, D], BF16)
                xn = [sbA(f"xn{i}", [128, D], BF16) for i in range(2)]
                wq_b = [sbA(f"wq_b{i}", [128, KT * 128], BF16) for i in range(2)]
                esb = sbA("esb", [128, 512], F32)
                rsb = sbA("rsb", [128, 512], F32)
                sig_bf = [sbA(f"sig_bf{i}", [128, 512], BF16) for i in range(2)]
                dnf = sbA("dnf", [128, 512], F32)
                if half == 0:
                    woraw = sbA("woraw", [128, D], F32)
                    wo_stage = sbA("wo_stage", [128, D], BF16)
                rcp_f = sbA("rcp_f", [128, 512], F32)
                denb = sbA("denb", [128, 512], BF16)

                ops = mkops()
                ln_base = C["e_ln"]
                lnr_base = C["e_lnr"]
                vtr_base = C["e_vtr"]
                ttr_base = C["e_ttr"]

                for tloc in range(NTILE):
                    T = half * NTILE + tloc

                    def f_x(sy, T=T, tloc=tloc, need=ln_base + tloc - 1):
                        if tloc >= 2:
                            sy.wait_ge(S["e_ln"], need)
                        sy.dma_start(out=x_t[tloc % 2][:, :],
                                     in_=x_in[T * 128:(T + 1) * 128, :]).then_inc(S[f"sxt{T % 2}"], 16)
                    ops["sync"].append(f_x)
                    n_xt += 1

                for tloc in range(NTILE):
                    Tg = half * NTILE + tloc
                    xt_sem = f"sxt{Tg % 2}"
                    xt_need = 16 * (Tg // 2 + 1)

                    def s_sq(s, tloc=tloc, xt_sem=xt_sem, xt_need=xt_need, lnr_need=lnr_base + tloc):
                        s.wait_ge(S[xt_sem], xt_need)
                        if tloc >= 1:
                            s.wait_ge(S["e_lnr"], lnr_need)
                        s.activation(xsq[:, :], x_t[tloc % 2][:, :],
                                     AF.Square).then_inc(S["e_sg"], 1)
                    ops["scalar"].append(s_sq)
                    C["e_sg"] += 1
                    sq_done = C["e_sg"]

                    def v_st(v, tloc=tloc, xt_sem=xt_sem, xt_need=xt_need, sq_done=sq_done):
                        v.wait_ge(S[xt_sem], xt_need)
                        xt = x_t[tloc % 2]
                        v.tensor_reduce(c_nsum[:, :], xt[:, :], AXX, ALU.add, negate=True)
                        v.tensor_scalar(c_nm[:, :], c_nsum[:, :], c_invd[:, 0:1], None, ALU.mult)
                        v.wait_ge(S["e_sg"], sq_done)
                        v.tensor_reduce(c_ssq[:, :], xsq[:, :], AXX, ALU.add).then_inc(S["e_lnr"], 1)
                        v.tensor_scalar(c_nm2n[:, :], c_nm[:, :], c_nm[:, 0:1], c_neg1[:, 0:1],
                                        ALU.mult, ALU.mult)
                        v.tensor_scalar(c_ssq[:, :], c_ssq[:, :], c_invd[:, 0:1], None, ALU.mult)
                        v.tensor_copy(c_junk[:, :], c_one[:, :])
                        v.tensor_scalar(c_var[:, :], c_ssq[:, :], c_nm2n[:, 0:1],
                                        None, ALU.add).then_inc(S["e_vr"], 1)
                    ops["vector"].append(v_st)
                    C["e_lnr"] += 1
                    C["e_vr"] += 1
                    var_done = C["e_vr"]

                    def s_rt(s, var_done=var_done):
                        s.wait_ge(S["e_vr"], var_done)
                        s.activation(c_sig[:, :], c_var[:, :], AF.Sqrt,
                                     bias=c_eps5[:, 0:1]).then_inc(S["e_sg"], 1)
                    ops["scalar"].append(s_rt)
                    C["e_sg"] += 1
                    sig_done = C["e_sg"]

                    def v_xn(v, tloc=tloc, sig_done=sig_done,
                             ttr_need=ttr_base + 32 * (tloc - 1)):
                        v.wait_ge(S["e_sg"], sig_done)
                        v.reciprocal(c_inv[:, :], c_sig[:, :])
                        v.tensor_copy(c_junk[:, :], c_one[:, :])
                        if tloc >= 2:
                            v.wait_ge(S["e_ttr"], ttr_need)
                        v.tensor_scalar(xn[tloc % 2][:, :], x_t[tloc % 2][:, :],
                                        c_nm[:, 0:1], c_inv[:, 0:1],
                                        ALU.add, ALU.mult).then_inc(S["e_ln"], 1)
                    ops["vector"].append(v_xn)
                    C["e_ln"] += 1
                    xn_done = C["e_ln"]

                    def t_tr(t, tloc=tloc, xn_done=xn_done, vtr_b=vtr_base + 32 * tloc,
                             first=(half == 0 and tloc == 0)):
                        t.wait_ge(S["e_ln"], xn_done)
                        for k in range(KT):
                            gk = vtr_b - vtr_base + k
                            if gk >= 2:
                                t.wait_ge(S["e_vtr"], vtr_b + k - 1)
                            t.matmul(ps_tr[k % 2][:, :], xn[tloc % 2][:, k * 128:(k + 1) * 128],
                                     ident_bf[:, :], start=True,
                                     stop=True).then_inc(S["e_ttr"], 1)
                    ops["tensor"].append(t_tr)
                    C["e_ttr"] += KT

                    def v_tr(v, tloc=tloc, ttr_b=ttr_base + 32 * tloc):
                        for k in range(KT):
                            v.wait_ge(S["e_ttr"], ttr_b + k + 1)
                            v.tensor_copy(
                                xnT[:, k * HALF + tloc * 128: k * HALF + (tloc + 1) * 128],
                                ps_tr[k % 2][:, :]).then_inc(S["e_vtr"], 1)
                    ops["vector"].append(v_tr)
                    C["e_vtr"] += KT

                all_tr = C["e_vtr"]
                base_tq = C["e_tq"]
                do_attn = stop not in ('ln0',)
                a_stage = 5
                if stop.startswith('a1h'):
                    a_stage = int(stop[4:])

                mlist = list(range(H)) if do_attn else []
                if stop.startswith('a1h'):
                    mlist = [0]
                wq_cnt_of = {}

                def emit_wq(m):
                    def f_wq(sy, m=m, tq_need=base_tq + 2 * (m - 1)):
                        if m >= 2:
                            sy.wait_ge(S["e_tq"], tq_need)
                        else:
                            sy.wait_ge(S["swqs"], 128)
                        sy.dma_start(
                            out=wq_b[m % 2][:, :].rearrange("p (k q) -> p k q", k=KT),
                            in_=wqT_s.rearrange("k p q -> p k q")[:, :, m * 128:(m + 1) * 128],
                        ).then_inc(S[f"swq{m % 2}"], 16)
                    ops["sync"].append(f_wq)

                def emit_wo_transform(g_idx):
                    for r in range(4):
                        for s_q in range(4):
                            sem = f"sacc{s_q}"
                            def f_acc(g, g_idx=g_idx, r=r, s_q=s_q, sem=sem,
                                      prev=C[sem]):
                                if prev:
                                    g.wait_ge(S[sem], prev)
                                if r == 0 and g_idx:
                                    g.wait_ge(S["swos"], 16 * g_idx)
                                cols = slice(s_q * (D // 4), (s_q + 1) * (D // 4))
                                srcw = woTb.rearrange("r (g p) f -> r p g f", p=128)[r][:, g_idx, cols]
                                if r == 0:
                                    g.dma_start(out=woraw[:, cols], in_=srcw).then_inc(S[sem], 16)
                                else:
                                    g.dma_start(out=woraw[:, cols], in_=srcw,
                                                accum_op=ALU.add).then_inc(S[sem], 16)
                            ops["gpsimd"].append(f_acc)
                            C[sem] += 16

                    def f_cast(g, targs=tuple((f"sacc{i}", C[f"sacc{i}"]) for i in range(4)),
                               first=(g_idx == 0)):
                        if first:
                            g.wait_ge(S["swqs"], 128)
                        for sname, val in targs:
                            g.wait_ge(S[sname], val)
                        g.tensor_copy(wo_stage[:, :], woraw[:, :])
                    ops["gpsimd"].append(f_cast)

                    def f_wos(g, g_idx=g_idx):
                        g.dma_start(out=woT_s[g_idx], in_=wo_stage[:, :]).then_inc(S["swos"], 16)
                    ops["gpsimd"].append(f_wos)
                    C["swos"] += 16

                if mlist:
                    if half == 0:
                        def f_strips(g):
                            for s in range(8):
                                if s >= 1:
                                    g.wait_ge(S["swqs"], 16 * s)
                                g.dma_start(
                                    out=wo_stage[:, :].rearrange("p (k q) -> p k q", k=4),
                                    in_=wqT.rearrange("(s k p) q -> s p k q",
                                                      s=8, p=128)[s],
                                ).then_inc(S["swqc"], 16)
                                g.wait_ge(S["swqc"], 16 * (s + 1))
                                g.dma_start(
                                    out=wqT_s[4 * s:4 * s + 4].rearrange("k p q -> p k q"),
                                    in_=wo_stage[:, :].rearrange("p (k q) -> p k q", k=4),
                                ).then_inc(S["swqs"], 16)
                        ops["gpsimd"].append(f_strips)
                        C["swqc"] += 128
                        C["swqs"] += 128
                        for g_idx in range(H):
                            emit_wo_transform(g_idx)
                    for m in mlist:
                        emit_wq(m)
                    for m in mlist:
                        n_wq += 1
                        wq_cnt_of[m] = (f"swq{m % 2}", 16 * ((half * H + m) // 2 + 1))

                for m in mlist:
                    for n in (range(NSEQ) if not stop.startswith('a1h') else [0]):
                        s_idx = C["e_tq"]

                        def t_q(t, m=m, n=n, s_idx=s_idx, all_tr=all_tr, base_tq=base_tq,
                                wq_need=wq_cnt_of[m]):
                            if m == 0 and n == 0:
                                t.wait_ge(S["e_vtr"], all_tr)
                            t.wait_ge(S[wq_need[0]], wq_need[1])
                            if s_idx - 2 >= base_tq:
                                t.wait_ge(S["e_sg"], sg_of[s_idx - 2])
                            ins = None
                            for k in range(KT):
                                ins = t.matmul(ps_q[n][:, :], wq_b[m % 2][:, k * 128:(k + 1) * 128],
                                               xnT[:, k * HALF + n * 512: k * HALF + (n + 1) * 512],
                                               start=(k == 0), stop=(k == KT - 1))
                            ins.then_inc(S["e_tq"], 1)
                        ops["tensor"].append(t_q)
                        C["e_tq"] += 1
                        if a_stage < 2:
                            continue
                        tq_done = C["e_tq"]

                        def s_sig(s, m=m, n=n, tq_done=tq_done):
                            s.wait_ge(S["e_tq"], tq_done)
                            s.activation(esb[:, :], ps_q[n][:, :], AF.Exp, bias=bq_g[m][:, 0:1])
                            s.activation(rsb[:, :], ps_q[n][:, :], AF.Relu,
                                         bias=bq_g[m][:, 0:1]).then_inc(S["e_sg"], 1)
                        ops["scalar"].append(s_sig)
                        C["e_sg"] += 1
                        sg_of[s_idx] = C["e_sg"]
                        if a_stage < 3:
                            continue

                        def v_sig(v, n=n, s_idx=s_idx, base_tq=base_tq, ready=C["e_sg"]):
                            v.wait_ge(S["e_sg"], ready)
                            if s_idx - 2 >= base_tq:
                                v.wait_ge(S["e_tn"], tn_of[s_idx - 2])
                            v.tensor_scalar(esb[:, :], esb[:, :], c_one[:, 0:1], None, ALU.min)
                            v.tensor_copy(c_junk[:, :], c_one[:, :])
                            v.tensor_tensor(sig_bf[n][:, :], rsb[:, :], esb[:, :],
                                            ALU.add).then_inc(S["e_vsig"], 1)
                        ops["vector"].append(v_sig)
                        C["e_vsig"] += 1
                        if a_stage < 4:
                            continue

                        def t_nd(t, m=m, n=n, s_idx=s_idx, base_tq=base_tq,
                                 vsig_done=C["e_vsig"]):
                            t.wait_ge(S["e_vsig"], vsig_done)
                            if s_idx - 1 >= base_tq:
                                t.wait_ge(S["e_vretr"], retr_of[s_idx - 1])
                            t.matmul(ps_n[n][:, :], m_bf[:, m * 128:(m + 1) * 128],
                                     sig_bf[n][:, :], start=True, stop=True)
                            t.matmul(ps_d[:, :], zrep[:, m * 128:(m + 1) * 128],
                                     sig_bf[n][:, :], start=True,
                                     stop=True).then_inc(S["e_tn"], 1)
                        ops["tensor"].append(t_nd)
                        C["e_tn"] += 1
                        tn_of[s_idx] = C["e_tn"]
                        if a_stage < 5:
                            continue

                        def v_retr(v, m=m, n=n, nd_done=C["e_tn"]):
                            v.wait_ge(S["e_tn"], nd_done)
                            v.tensor_scalar(dnf[:, :], ps_d[:, :], c_eps6[:, 0:1],
                                            None, ALU.add)
                            v.tensor_copy(c_junk[:, :], c_one[:, :])
                            v.reciprocal(rcp_f[:, :], dnf[:, :])
                            v.tensor_copy(c_junk[:, :], c_one[:, :])
                            v.tensor_copy(denb[:, :], rcp_f[:, :])
                            v.tensor_copy(c_junk[:, :], c_one[:, :])
                            v.tensor_tensor(
                                retrT[:, m * HALF + n * 512: m * HALF + (n + 1) * 512],
                                ps_n[n][:, :], denb[:, :], ALU.mult).then_inc(S["e_vretr"], 1)
                        ops["vector"].append(v_retr)
                        C["e_vretr"] += 1
                        retr_of[s_idx] = C["e_vretr"]

                emit(ops)
            barrier()
            if stop in ('ln0', 'a0') or stop.startswith('a1h'):
                continue

            with ExitStack() as esB:
                def sbB(name, shape, dt):
                    return esB.enter_context(nc.sbuf_tensor(f"{name}_h{half}", shape, dt))
                wdq = [sbB(f"wdq{i}", [128, KT * CW], BF16) for i in range(2)]
                w8b = [sbB(f"w8b{i}", [128, KT * CW], I8) for i in range(2)]
                scb = [sbB(f"scb{i}", [128, 4 * CW], F32) for i in range(4)]
                wobf = [sbB(f"wobf{i}", [128, H * CW], BF16) for i in range(2)]
                xrb = [sbB(f"xrb{i}", [128, NTILE * CW], F32) for i in range(2)]
                ost = [sbB(f"ost{i}", [128, NTILE * CW], F32) for i in range(2)]

                ops = mkops()
                ps_b4 = [ps_q[0], ps_q[1], ps_n[0], ps_n[1]]
                tf_base = C["e_tf"]
                ev_base = C["e_ev"]
                dqc_base = C["e_dqp"] // 8
                out_base = n_out
                xr_base = n_xr
                retr_ready = C["e_vretr"]

                def emit_evacs(v, c, half=half, ev_cnt_base=ev_base, tf_b=tf_base,
                               xr_b=xr_base, out_b=out_base):
                    cg = half * NCH + c
                    for tt in range(NTILE):
                        v.wait_ge(S["e_tf"], tf_b + c * NTILE + tt + 1)
                        if tt == 0:
                            v.wait_ge(S[f"sxr{c % 2}"], 16 * (cg // 2 + 1))
                            if c >= 2:
                                v.wait_ge(S[f"sout{c % 2}"], 16 * (cg // 2))
                        v.tensor_tensor(ost[c % 2][:, tt * CW:(tt + 1) * CW],
                                        ps_b4[tt % 4][:, 0:CW],
                                        xrb[c % 2][:, tt * CW:(tt + 1) * CW],
                                        ALU.add).then_inc(S["e_ev"], 1)

                for c in range(NCH):
                    cc0 = c * CW

                    def sy_w8(sy, c=c, cc0=cc0, dq_need=8 * (dqc_base + c - 1)):
                        if c >= 2:
                            sy.wait_ge(S["e_dqp"], dq_need)
                        sy.dma_start(
                            out=w8b[c % 2][:, :].rearrange("p (k j) -> p k j", k=KT),
                            in_=w8.rearrange("(k p) f -> p k f", p=128)[:, :, cc0:cc0 + CW],
                        ).then_inc(S[f"sw8{c % 2}"], 16)
                    ops["sync"].append(sy_w8)
                    n_w8 += 1

                    for kg in range(8):
                        def sy_sc(sy, c=c, cc0=cc0, kg=kg, pidx=n_sc):
                            if pidx >= 4:
                                sy.wait_ge(S["e_dqp"], pidx - 3)
                            sy.dma_start(
                                out=scb[pidx % 4][:, :].rearrange("p (kk j) -> p kk j", kk=4),
                                in_=wsc.rearrange("(kg kk p) f -> kg p kk f",
                                                  kg=8, p=128)[kg][:, :, cc0:cc0 + CW],
                            ).then_inc(S[f"ssc{pidx % 4}"], 16)
                        ops["sync"].append(sy_sc)
                        n_sc += 1

                    def sy_xr(sy, c=c, cc0=cc0, half=half, ev_need=ev_base + 8 * (c - 1)):
                        if c >= 2:
                            sy.wait_ge(S["e_ev"], ev_need)
                        sy.dma_start(
                            out=xrb[c % 2][:, :].rearrange("p (tt j) -> p tt j", tt=NTILE),
                            in_=x_in.rearrange("(hh tt p) f -> hh tt p f", hh=2, p=128)[half]
                                    .rearrange("tt p f -> p tt f")[:, :, cc0:cc0 + CW],
                        ).then_inc(S[f"sxr{c % 2}"], 16)
                    ops["sync"].append(sy_xr)
                    n_xr += 1

                    def sy_wot(sy, c=c, cc0=cc0, half=half, tf_need=tf_base + 8 * (c - 1)):
                        if half == 0 and c == 0:
                            sy.wait_ge(S["swos"], 16 * H)
                        if c >= 2:
                            sy.wait_ge(S["e_tf"], tf_need)
                        sy.dma_start(
                            out=wobf[c % 2][:, :].rearrange("p (g j) -> p g j", g=H),
                            in_=woT_s.rearrange("g p f -> p g f")[:, :, cc0:cc0 + CW],
                        ).then_inc(S[f"swot{c % 2}"], 16)
                    if not nowo:
                        ops["gpsimd"].append(sy_wot)

                    if c >= 1:
                        def g_out(g, cprev=c - 1, half=half, ev_need=ev_base + 8 * c,
                                  so=n_out):
                            g.wait_ge(S["e_ev"], ev_need)
                            if so >= 2:
                                g.wait_ge(S[f"sout{so % 2}"], 16 * (so // 2))
                            g.dma_start(
                                out=out.rearrange("(hh tt p) f -> hh tt p f", hh=2, p=128)[half]
                                       .rearrange("tt p f -> p tt f")[:, :, cprev * CW:(cprev + 1) * CW],
                                in_=ost[cprev % 2][:, :].rearrange("p (tt j) -> p tt j", tt=NTILE),
                            ).then_inc(S[f"sout{so % 2}"], 16)
                        ops["gpsimd"].append(g_out)
                        n_out += 1

                    def v_dq(v, c=c, w8_sem=f"sw8{c % 2}", w8_need=16 * ((half * NCH + c) // 2 + 1),
                             piece0=n_sc - 8, tf_need=tf_base + 8 * (c - 1)):
                        v.wait_ge(S[w8_sem], w8_need)
                        if c >= 2:
                            v.wait_ge(S["e_tf"], tf_need)
                        for k in range(KT):
                            kg = k // 4
                            if k % 4 == 0:
                                pp = piece0 + kg
                                v.wait_ge(S[f"ssc{pp % 4}"], 16 * (pp // 4 + 1))
                            pidx = piece0 + kg
                            ins = v.tensor_tensor(wdq[c % 2][:, k * CW:(k + 1) * CW],
                                                  w8b[c % 2][:, k * CW:(k + 1) * CW],
                                                  scb[pidx % 4][:, (k % 4) * CW:(k % 4 + 1) * CW],
                                                  ALU.mult)
                            if k % 4 == 3:
                                ins.then_inc(S["e_dqp"], 1)
                    ops["vector"].append(v_dq)
                    C["e_dqp"] += 8

                    if c >= 1:
                        def v_ev(v, cprev=c - 1):
                            emit_evacs(v, cprev)
                        ops["vector"].append(v_ev)
                        C["e_ev"] += NTILE

                    def t_ch(t, c=c, dq_need=8 * (dqc_base + c + 1),
                             wo_sem=f"swot{c % 2}", wo_need=16 * ((half * NCH + c) // 2 + 1),
                             ev_b=ev_base, retr_ready=retr_ready):
                        if c == 0:
                            t.wait_ge(S["e_vretr"], retr_ready)
                        t.wait_ge(S["e_dqp"], dq_need)
                        if not nowo:
                            t.wait_ge(S[wo_sem], wo_need)
                        for tt in range(NTILE):
                            gi = c * NTILE + tt
                            if gi >= 4:
                                t.wait_ge(S["e_ev"], ev_b + gi - 3)
                            psq = ps_b4[tt % 4]
                            for k in range(KT):
                                t.matmul(psq[:, 0:CW],
                                         xnT[:, k * HALF + tt * 128: k * HALF + (tt + 1) * 128],
                                         wdq[c % 2][:, k * CW:(k + 1) * CW],
                                         start=(k == 0), stop=False)
                            if not (nowo or nowomm):
                                for g in range(H):
                                    t.matmul(psq[:, 0:CW],
                                             retrT[:, g * HALF + tt * 128: g * HALF + (tt + 1) * 128],
                                             wobf[c % 2][:, g * CW:(g + 1) * CW],
                                             start=False, stop=False)
                            t.matmul(psq[:, 0:CW], ones_sq[:, :], b_rep[:, c * CW:(c + 1) * CW],
                                     start=False, stop=True).then_inc(S["e_tf"], 1)
                    ops["tensor"].append(t_ch)
                    C["e_tf"] += NTILE

                def v_ev_last(v, cprev=NCH - 1):
                    emit_evacs(v, cprev)
                ops["vector"].append(v_ev_last)
                C["e_ev"] += NTILE

                def g_out_last(g, cprev=NCH - 1, half=half, ev_need=ev_base + 8 * NCH,
                               so=n_out):
                    g.wait_ge(S["e_ev"], ev_need)
                    if so >= 2:
                        g.wait_ge(S[f"sout{so % 2}"], 16 * (so // 2))
                    g.dma_start(
                        out=out.rearrange("(hh tt p) f -> hh tt p f", hh=2, p=128)[half]
                               .rearrange("tt p f -> p tt f")[:, :, cprev * CW:(cprev + 1) * CW],
                        in_=ost[cprev % 2][:, :].rearrange("p (tt j) -> p tt j", tt=NTILE),
                    ).then_inc(S[f"sout{so % 2}"], 16)
                ops["gpsimd"].append(g_out_last)
                n_out += 1

                emit(ops)
            barrier()

        if n_out:
            ops = mkops()
            t0v = 16 * ((n_out + 1) // 2)
            t1v = 16 * (n_out // 2)
            def g_fin(g, t0v=t0v, t1v=t1v):
                g.wait_ge(S["sout0"], t0v)
                g.wait_ge(S["sout1"], t1v)
            ops["gpsimd"].append(g_fin)
            emit(ops)

    return nc


_NC_CACHE = {}


def kernel(L0_attn_hidden_states, L0_attn_Wq, L0_attn_bq, L0_attn_Wk, L0_attn_bk,
           L0_attn_Wv, L0_attn_bv, L0_attn_Wo, L0_attn_M, L0_attn_z,
           L0_ffn_w, L0_ffn_w_scale, L0_ffn_b):
    x = np.ascontiguousarray(np.asarray(L0_attn_hidden_states, dtype=np.float32))
    Wq = np.asarray(L0_attn_Wq, dtype=np.float32)
    bq = np.asarray(L0_attn_bq, dtype=np.float32)
    Wo = np.asarray(L0_attn_Wo, dtype=np.float32)
    M = np.ascontiguousarray(np.asarray(L0_attn_M, dtype=np.float32))
    z = np.ascontiguousarray(np.asarray(L0_attn_z, dtype=np.float32))
    w8 = np.ascontiguousarray(np.asarray(L0_ffn_w, dtype=np.int8))
    wsc = np.ascontiguousarray(np.asarray(L0_ffn_w_scale, dtype=np.float32))
    bff = np.ascontiguousarray(np.asarray(L0_ffn_b, dtype=np.float32).reshape(1, D))

    Wq_sel = Wq.reshape(8, 4, 128, D)[:, 0].reshape(8 * 128, D)
    wqT = np.ascontiguousarray(Wq_sel.T)
    bq2 = np.ascontiguousarray(bq.reshape(8, 4, 128)[:, 0])
    woTb = np.ascontiguousarray(
        Wo.T.reshape(8, 4, 128, D).transpose(1, 0, 2, 3).reshape(4, 1024, D))
    ident = np.eye(128, dtype=np.float32)

    if "nc" not in _NC_CACHE:
        _NC_CACHE["nc"] = build_program()
    nc = _NC_CACHE["nc"]

    in_maps = []
    for i in range(8):
        in_maps.append(dict(x_in=np.ascontiguousarray(x[i]), wqT=wqT, bq2=bq2,
                            m_in=M, z_in=z, woTb=woTb, w8=w8, wsc=wsc, bff=bff,
                            ident=ident))
    res = run_bass_kernel_spmd(nc, in_maps, core_ids=list(range(8)))
    return np.stack([r["out"] for r in res.results], axis=0)


# version 13
# speedup vs baseline: 1.3569x; 1.0200x over previous
import numpy as np
from contextlib import ExitStack

import concourse.bass as bass
import concourse.mybir as mybir
from concourse.bass_utils import run_bass_kernel_spmd

F32 = mybir.dt.float32
BF16 = mybir.dt.bfloat16
I8 = mybir.dt.int8
AF = mybir.ActivationFunctionType
ALU = mybir.AluOpType
AXX = mybir.AxisListType.X

D = 4096
KT = 32
H = 8
CW = 256
NCH = D // CW
TOK = 2048
HALF = TOK // 2
NTILE = HALF // 128
NSEQ = HALF // 512

ENGS = ("sync", "gpsimd", "scalar", "vector", "tensor")


def build_program(stop='full', nowo=False, nowomm=False):
    nc = bass.Bass("TRN2", target_bir_lowering=False)

    x_in = nc.dram_tensor("x_in", [TOK, D], F32, kind="ExternalInput")
    wqT = nc.dram_tensor("wqT", [D, H * 128], F32, kind="ExternalInput")
    bq2 = nc.dram_tensor("bq2", [H, 128], F32, kind="ExternalInput")
    m_in = nc.dram_tensor("m_in", [H, 128, 128], F32, kind="ExternalInput")
    z_in = nc.dram_tensor("z_in", [H, 128], F32, kind="ExternalInput")
    woTb = nc.dram_tensor("woTb", [4, H * 128, D], F32, kind="ExternalInput")
    w8 = nc.dram_tensor("w8", [D, D], I8, kind="ExternalInput")
    wsc = nc.dram_tensor("wsc", [D, D], F32, kind="ExternalInput")
    bff = nc.dram_tensor("bff", [1, D], F32, kind="ExternalInput")
    ident = nc.dram_tensor("ident", [128, 128], F32, kind="ExternalInput")
    out = nc.dram_tensor("out", [TOK, D], F32, kind="ExternalOutput")
    woT_s = nc.dram_tensor("woT_s", [H, 128, D], BF16, kind="Internal")
    wqT_s = nc.dram_tensor("wqT_s", [KT, 128, H * 128], BF16, kind="Internal")

    with ExitStack() as es:
        def sbuf(name, shape, dt):
            return es.enter_context(nc.sbuf_tensor(name, shape, dt))

        xnT = sbuf("xnT", [128, KT * HALF], BF16)
        retrT = sbuf("retrT", [128, H * HALF], BF16)
        m_bf = sbuf("m_bf", [128, H * 128], BF16)
        zrep = sbuf("zrep", [128, H * 128], BF16)
        b_rep = sbuf("b_rep", [128, D], BF16)
        ident_bf = sbuf("ident_bf", [128, 128], BF16)
        ones_sq = sbuf("ones_sq", [128, 128], BF16)
        onesf = sbuf("onesf", [128, 128], F32)
        z_g = [sbuf(f"z_g{g}", [128, 1], F32) for g in range(H)]
        bq_g = [sbuf(f"bq_g{g}", [128, 1], F32) for g in range(H)]
        c_eps5 = sbuf("c_eps5", [128, 1], F32)
        c_eps6 = sbuf("c_eps6", [128, 1], F32)
        c_invd = sbuf("c_invd", [128, 1], F32)
        c_neg1 = sbuf("c_neg1", [128, 1], F32)
        c_one = sbuf("c_one", [128, 1], F32)
        c_i128 = sbuf("c_i128", [128, 1], F32)
        c_junk = sbuf("c_junk", [128, 1], F32)
        c_nsum = sbuf("c_nsum", [128, 1], F32)
        c_nm = sbuf("c_nm", [128, 1], F32)
        c_ssq = sbuf("c_ssq", [128, 1], F32)
        c_nm2n = sbuf("c_nm2n", [128, 1], F32)
        c_var = sbuf("c_var", [128, 1], F32)
        c_sig = sbuf("c_sig", [128, 1], F32)
        c_inv = sbuf("c_inv", [128, 1], F32)

        ps_q = [es.enter_context(nc.psum_tensor(f"ps_q{i}", [128, 512], F32)) for i in range(2)]
        ps_n = [es.enter_context(nc.psum_tensor(f"ps_n{i}", [128, 512], F32)) for i in range(2)]
        ps_b = es.enter_context(nc.psum_tensor("ps_b", [128, 512], F32))
        ps_d = es.enter_context(nc.psum_tensor("ps_d", [128, 512], F32))
        ps_tr = [es.enter_context(nc.psum_tensor(f"ps_tr{i}", [128, 128], F32)) for i in range(2)]

        sem_names = ["sxt0", "sxt1", "swq0", "swq1", "sw80", "sw81", "ssc0", "ssc1", "ssc2", "ssc3",
                     "sxr0", "sxr1", "sout0", "sout1", "sacc0", "sacc1", "sacc2", "sacc3", "swos",
                     "swot0", "swot1", "scst", "swqs", "swqc",
                     "e_g", "e_ln", "e_lnr", "e_vtr", "e_ttr",
                     "e_tq", "e_sg", "e_vsig", "e_tn", "e_vr", "e_tb", "e_vretr",
                     "e_dqc", "e_dqp", "e_vwoc", "e_ev", "e_tf", "ph"]
        S = {n: es.enter_context(nc.semaphore(n)) for n in sem_names}
        C = {n: 0 for n in sem_names}

        n_xt = n_wq = n_w8 = n_sc = n_xr = n_out = n_acc = 0

        block = es.enter_context(nc.Block())

        def emit(ops):
            for eng in ENGS:
                lst = ops.get(eng)
                if not lst:
                    continue
                getattr(block, eng)(lambda h, lst=lst: [f(h) for f in lst])

        def mkops():
            return {e: [] for e in ENGS}

        def barrier():
            ops = mkops()
            C["ph"] += 5
            tgt = C["ph"]
            pre = {
                "tensor": [("e_vretr", C["e_vretr"]), ("e_ev", C["e_ev"]),
                           ("e_vtr", C["e_vtr"]), ("e_vsig", C["e_vsig"])],
                "gpsimd": [("sout0", 16 * ((n_out + 1) // 2)), ("sout1", 16 * (n_out // 2)),
                           ("sacc0", C["sacc0"]), ("sacc1", C["sacc1"]),
                           ("sacc2", C["sacc2"]), ("sacc3", C["sacc3"]), ("swos", C["swos"])],
                "vector": [("e_tf", C["e_tf"])],
                "scalar": [],
                "sync": [],
            }
            for eng in ENGS:
                def f(h, t=tgt, waits=tuple(pre[eng])):
                    for sname, val in waits:
                        if val:
                            h.wait_ge(S[sname], val)
                    h.nop().then_inc(S["ph"], 1)
                    h.wait_ge(S["ph"], t)
                ops[eng].append(f)
            emit(ops)

        sg_of, tn_of, vr_of, retr_of = {}, {}, {}, {}

        if stop == 'bar':
            barrier()
            return nc
        es_pro = ExitStack()
        bcol = [es_pro.enter_context(nc.sbuf_tensor(f"bcol{j}", [128, 1], F32))
                for j in range(KT)]
        rep_f = es_pro.enter_context(nc.sbuf_tensor("rep_f", [128, 128], F32))
        rep_b = es_pro.enter_context(nc.sbuf_tensor("rep_b", [128, 128], BF16))
        ops = mkops()

        def g_pro(g):
            g.memset(ones_sq[:, :], 1.0)
            g.memset(onesf[:, :], 1.0)
            g.memset(c_eps5[:, :], 1e-5)
            g.memset(c_eps6[:, :], 1e-6)
            g.memset(c_invd[:, :], 1.0 / D)
            g.memset(c_neg1[:, :], -1.0)
            g.memset(c_one[:, :], 1.0)
            g.memset(c_i128[:, :], 1.0 / 128.0)
            g.memset(c_junk[:, :], 0.0).then_inc(S["e_g"], 1)
            g.dma_start(out=m_bf[:, :].rearrange("p (g e) -> p g e", g=H),
                        in_=m_in.rearrange("g d e -> d g e")).then_inc(S["scst"], 16)
            g.dma_start(out=ident_bf[:, :], in_=ident[:, :]).then_inc(S["scst"], 16)
            for j in range(KT):
                g.dma_start(out=bcol[j][:, :],
                            in_=bff.rearrange("o f -> f o")[j * 128:(j + 1) * 128, :],
                            allow_slow_non_contiguous=True).then_inc(S["scst"], 16)
            for gg in range(H):
                g.dma_start(out=z_g[gg][:, :], in_=z_in.rearrange("g d -> d g")[:, gg:gg + 1],
                            allow_slow_non_contiguous=True).then_inc(S["scst"], 16)
                g.dma_start(out=bq_g[gg][:, :], in_=bq2.rearrange("g d -> d g")[:, gg:gg + 1],
                            allow_slow_non_contiguous=True).then_inc(S["scst"], 16)
        ops["gpsimd"].append(g_pro)
        C["e_g"] += 1
        C["scst"] += 16 * (2 + KT + 2 * H)
        cst_done = C["scst"]
        if stop == 'pro_g':
            emit(ops)
            es_pro.close()
            return nc

        def s_pro(s, cst_done=cst_done):
            s.wait_ge(S["e_g"], 1)
            s.wait_ge(S["scst"], cst_done)
            ins = None
            for gg in range(H):
                ins = s.activation(zrep[:, gg * 128:(gg + 1) * 128], onesf[:, :], AF.Copy,
                                   scale=z_g[gg][:, 0:1])
            ins.then_inc(S["e_sg"], 1)
        ops["scalar"].append(s_pro)
        C["e_sg"] += 1
        if stop == 'pro_s':
            emit(ops)
            es_pro.close()
            return nc

        def v_pro(v, cst_done=cst_done):
            v.wait_ge(S["e_g"], 1)
            v.wait_ge(S["scst"], cst_done)
            ins = None
            for j in range(KT):
                ins = v.tensor_scalar(bcol[j][:, :], bcol[j][:, :], c_i128[:, 0:1], None, ALU.mult)
            ins.then_inc(S["e_ln"], 1)
        ops["vector"].append(v_pro)
        C["e_ln"] += 1
        if stop == 'pro_v':
            emit(ops)
            es_pro.close()
            return nc

        for j in range(KT):
            def s_bj(s, j=j, v_need=C["e_vsig"] + 0, ln_need=C["e_ln"]):
                if j == 0:
                    s.wait_ge(S["e_ln"], ln_need)
                else:
                    s.wait_ge(S["e_vsig"], v_need)
                s.activation(rep_f[:, :], onesf[:, :], AF.Copy,
                             scale=bcol[j][:, 0:1]).then_inc(S["e_sg"], 1)
            ops["scalar"].append(s_bj)
            C["e_sg"] += 1

            def v_bj(v, j=j, s_need=C["e_sg"], t_need=C["e_tb"]):
                v.wait_ge(S["e_sg"], s_need)
                if j >= 1:
                    v.wait_ge(S["e_tb"], t_need)
                v.tensor_copy(rep_b[:, :], rep_f[:, :]).then_inc(S["e_vsig"], 1)
            ops["vector"].append(v_bj)
            C["e_vsig"] += 1

            def t_bj(t, j=j, v_need=C["e_vsig"], e_need=C["e_vretr"]):
                if j == 0:
                    t.wait_ge(S["e_g"], 1)
                t.wait_ge(S["e_vsig"], v_need)
                if j >= 1:
                    t.wait_ge(S["e_vretr"], e_need)
                t.matmul(ps_b[:, 0:128], rep_b[:, :], ident_bf[:, :],
                         start=True, stop=True).then_inc(S["e_tb"], 1)
            ops["tensor"].append(t_bj)
            C["e_tb"] += 1

            def v_bj2(v, j=j, t_need=C["e_tb"]):
                v.wait_ge(S["e_tb"], t_need)
                v.tensor_copy(b_rep[:, j * 128:(j + 1) * 128], ps_b[:, 0:128]).then_inc(S["e_vretr"], 1)
            ops["vector"].append(v_bj2)
            C["e_vretr"] += 1

        emit(ops)
        es_pro.close()
        barrier()

        halves = [] if stop == 'pro' else ([0] if stop in ('ln0', 'a0', 'b0') else [0, 1])
        for half in halves:
            with ExitStack() as esA:
                def sbA(name, shape, dt):
                    return esA.enter_context(nc.sbuf_tensor(f"{name}_h{half}", shape, dt))
                x_t = [sbA(f"x_t{i}", [128, D], F32) for i in range(2)]
                xsq = sbA("xsq", [128, D], BF16)
                xn = [sbA(f"xn{i}", [128, D], BF16) for i in range(2)]
                wq_b = [sbA(f"wq_b{i}", [128, KT * 128], BF16) for i in range(2)]
                esb = sbA("esb", [128, 512], F32)
                rsb = sbA("rsb", [128, 512], F32)
                sig_bf = [sbA(f"sig_bf{i}", [128, 512], BF16) for i in range(2)]
                dnf = sbA("dnf", [128, 512], F32)
                if half == 0:
                    woraw = sbA("woraw", [128, D], F32)
                    wo_stage = sbA("wo_stage", [128, D], BF16)
                rcp_f = sbA("rcp_f", [128, 512], F32)
                denb = sbA("denb", [128, 512], BF16)

                ops = mkops()
                tr_ps4 = [ps_tr[0], ps_tr[1], ps_b, ps_d]
                ln_base = C["e_ln"]
                lnr_base = C["e_lnr"]
                vtr_base = C["e_vtr"]
                ttr_base = C["e_ttr"]

                for tloc in range(NTILE):
                    T = half * NTILE + tloc

                    def f_x(sy, T=T, tloc=tloc, need=ln_base + tloc - 1):
                        if tloc >= 2:
                            sy.wait_ge(S["e_ln"], need)
                        sy.dma_start(out=x_t[tloc % 2][:, :],
                                     in_=x_in[T * 128:(T + 1) * 128, :]).then_inc(S[f"sxt{T % 2}"], 16)
                    ops["sync"].append(f_x)
                    n_xt += 1

                for tloc in range(NTILE):
                    Tg = half * NTILE + tloc
                    xt_sem = f"sxt{Tg % 2}"
                    xt_need = 16 * (Tg // 2 + 1)

                    def s_sq(s, tloc=tloc, xt_sem=xt_sem, xt_need=xt_need, lnr_need=lnr_base + tloc):
                        s.wait_ge(S[xt_sem], xt_need)
                        if tloc >= 1:
                            s.wait_ge(S["e_lnr"], lnr_need)
                        s.activation(xsq[:, :], x_t[tloc % 2][:, :],
                                     AF.Square).then_inc(S["e_sg"], 1)
                    ops["scalar"].append(s_sq)
                    C["e_sg"] += 1
                    sq_done = C["e_sg"]

                    def v_st(v, tloc=tloc, xt_sem=xt_sem, xt_need=xt_need, sq_done=sq_done):
                        v.wait_ge(S[xt_sem], xt_need)
                        xt = x_t[tloc % 2]
                        v.tensor_reduce(c_nsum[:, :], xt[:, :], AXX, ALU.add, negate=True)
                        v.tensor_scalar(c_nm[:, :], c_nsum[:, :], c_invd[:, 0:1], None, ALU.mult)
                        v.wait_ge(S["e_sg"], sq_done)
                        v.tensor_reduce(c_ssq[:, :], xsq[:, :], AXX, ALU.add).then_inc(S["e_lnr"], 1)
                        v.tensor_scalar(c_nm2n[:, :], c_nm[:, :], c_nm[:, 0:1], c_neg1[:, 0:1],
                                        ALU.mult, ALU.mult)
                        v.tensor_scalar(c_ssq[:, :], c_ssq[:, :], c_invd[:, 0:1], None, ALU.mult)
                        v.tensor_copy(c_junk[:, :], c_one[:, :])
                        v.tensor_scalar(c_var[:, :], c_ssq[:, :], c_nm2n[:, 0:1],
                                        None, ALU.add).then_inc(S["e_vr"], 1)
                    ops["vector"].append(v_st)
                    C["e_lnr"] += 1
                    C["e_vr"] += 1
                    var_done = C["e_vr"]

                    def s_rt(s, var_done=var_done):
                        s.wait_ge(S["e_vr"], var_done)
                        s.activation(c_sig[:, :], c_var[:, :], AF.Sqrt,
                                     bias=c_eps5[:, 0:1]).then_inc(S["e_sg"], 1)
                    ops["scalar"].append(s_rt)
                    C["e_sg"] += 1
                    sig_done = C["e_sg"]

                    def v_xn(v, tloc=tloc, sig_done=sig_done,
                             ttr_need=ttr_base + 32 * (tloc - 1)):
                        v.wait_ge(S["e_sg"], sig_done)
                        v.reciprocal(c_inv[:, :], c_sig[:, :])
                        v.tensor_copy(c_junk[:, :], c_one[:, :])
                        if tloc >= 2:
                            v.wait_ge(S["e_ttr"], ttr_need)
                        v.tensor_scalar(xn[tloc % 2][:, :], x_t[tloc % 2][:, :],
                                        c_nm[:, 0:1], c_inv[:, 0:1],
                                        ALU.add, ALU.mult).then_inc(S["e_ln"], 1)
                    ops["vector"].append(v_xn)
                    C["e_ln"] += 1
                    xn_done = C["e_ln"]

                    def t_tr(t, tloc=tloc, xn_done=xn_done, vtr_b=vtr_base + 32 * tloc,
                             first=(half == 0 and tloc == 0)):
                        t.wait_ge(S["e_ln"], xn_done)
                        for k in range(KT):
                            gk = vtr_b - vtr_base + k
                            if gk >= 4:
                                t.wait_ge(S["e_vtr"], vtr_b + k - 3)
                            t.matmul(tr_ps4[k % 4][:, 0:128],
                                     xn[tloc % 2][:, k * 128:(k + 1) * 128],
                                     ident_bf[:, :], start=True,
                                     stop=True).then_inc(S["e_ttr"], 1)
                    ops["tensor"].append(t_tr)
                    C["e_ttr"] += KT

                    def v_tr(v, tloc=tloc, ttr_b=ttr_base + 32 * tloc):
                        for k in range(KT):
                            v.wait_ge(S["e_ttr"], ttr_b + k + 1)
                            v.tensor_copy(
                                xnT[:, k * HALF + tloc * 128: k * HALF + (tloc + 1) * 128],
                                tr_ps4[k % 4][:, 0:128]).then_inc(S["e_vtr"], 1)
                    ops["vector"].append(v_tr)
                    C["e_vtr"] += KT

                all_tr = C["e_vtr"]
                base_tq = C["e_tq"]
                do_attn = stop not in ('ln0',)
                a_stage = 5
                if stop.startswith('a1h'):
                    a_stage = int(stop[4:])

                mlist = list(range(H)) if do_attn else []
                if stop.startswith('a1h'):
                    mlist = [0]
                wq_cnt_of = {}

                def emit_wq(m):
                    def f_wq(sy, m=m, tq_need=base_tq + 2 * (m - 1)):
                        if m >= 2:
                            sy.wait_ge(S["e_tq"], tq_need)
                        else:
                            sy.wait_ge(S["swqs"], 128)
                        sy.dma_start(
                            out=wq_b[m % 2][:, :].rearrange("p (k q) -> p k q", k=KT),
                            in_=wqT_s.rearrange("k p q -> p k q")[:, :, m * 128:(m + 1) * 128],
                        ).then_inc(S[f"swq{m % 2}"], 16)
                    ops["sync"].append(f_wq)

                def emit_wo_transform(g_idx):
                    for r in range(4):
                        for s_q in range(4):
                            sem = f"sacc{s_q}"
                            def f_acc(g, g_idx=g_idx, r=r, s_q=s_q, sem=sem,
                                      prev=C[sem]):
                                if prev:
                                    g.wait_ge(S[sem], prev)
                                if r == 0 and g_idx:
                                    g.wait_ge(S["swos"], 16 * g_idx)
                                cols = slice(s_q * (D // 4), (s_q + 1) * (D // 4))
                                srcw = woTb.rearrange("r (g p) f -> r p g f", p=128)[r][:, g_idx, cols]
                                if r == 0:
                                    g.dma_start(out=woraw[:, cols], in_=srcw).then_inc(S[sem], 16)
                                else:
                                    g.dma_start(out=woraw[:, cols], in_=srcw,
                                                accum_op=ALU.add).then_inc(S[sem], 16)
                            ops["gpsimd"].append(f_acc)
                            C[sem] += 16

                    def f_cast(g, targs=tuple((f"sacc{i}", C[f"sacc{i}"]) for i in range(4)),
                               first=(g_idx == 0)):
                        if first:
                            g.wait_ge(S["swqs"], 128)
                        for sname, val in targs:
                            g.wait_ge(S[sname], val)
                        g.tensor_copy(wo_stage[:, :], woraw[:, :])
                    ops["gpsimd"].append(f_cast)

                    def f_wos(g, g_idx=g_idx):
                        g.dma_start(out=woT_s[g_idx], in_=wo_stage[:, :]).then_inc(S["swos"], 16)
                    ops["gpsimd"].append(f_wos)
                    C["swos"] += 16

                if mlist:
                    if half == 0:
                        def f_strips(g):
                            for s in range(8):
                                if s >= 1:
                                    g.wait_ge(S["swqs"], 16 * s)
                                g.dma_start(
                                    out=wo_stage[:, :].rearrange("p (k q) -> p k q", k=4),
                                    in_=wqT.rearrange("(s k p) q -> s p k q",
                                                      s=8, p=128)[s],
                                ).then_inc(S["swqc"], 16)
                                g.wait_ge(S["swqc"], 16 * (s + 1))
                                g.dma_start(
                                    out=wqT_s[4 * s:4 * s + 4].rearrange("k p q -> p k q"),
                                    in_=wo_stage[:, :].rearrange("p (k q) -> p k q", k=4),
                                ).then_inc(S["swqs"], 16)
                        ops["gpsimd"].append(f_strips)
                        C["swqc"] += 128
                        C["swqs"] += 128
                        for g_idx in range(H):
                            emit_wo_transform(g_idx)
                    for m in mlist:
                        emit_wq(m)
                    for m in mlist:
                        n_wq += 1
                        wq_cnt_of[m] = (f"swq{m % 2}", 16 * ((half * H + m) // 2 + 1))

                for m in mlist:
                    for n in (range(NSEQ) if not stop.startswith('a1h') else [0]):
                        s_idx = C["e_tq"]

                        def t_q(t, m=m, n=n, s_idx=s_idx, all_tr=all_tr, base_tq=base_tq,
                                wq_need=wq_cnt_of[m]):
                            if m == 0 and n == 0:
                                t.wait_ge(S["e_vtr"], all_tr)
                            t.wait_ge(S[wq_need[0]], wq_need[1])
                            if s_idx - 2 >= base_tq:
                                t.wait_ge(S["e_sg"], sg_of[s_idx - 2])
                            ins = None
                            for k in range(KT):
                                ins = t.matmul(ps_q[n][:, :], wq_b[m % 2][:, k * 128:(k + 1) * 128],
                                               xnT[:, k * HALF + n * 512: k * HALF + (n + 1) * 512],
                                               start=(k == 0), stop=(k == KT - 1))
                            ins.then_inc(S["e_tq"], 1)
                        ops["tensor"].append(t_q)
                        C["e_tq"] += 1
                        if a_stage < 2:
                            continue
                        tq_done = C["e_tq"]

                        def s_sig(s, m=m, n=n, tq_done=tq_done):
                            s.wait_ge(S["e_tq"], tq_done)
                            s.activation(esb[:, :], ps_q[n][:, :], AF.Exp, bias=bq_g[m][:, 0:1])
                            s.activation(rsb[:, :], ps_q[n][:, :], AF.Relu,
                                         bias=bq_g[m][:, 0:1]).then_inc(S["e_sg"], 1)
                        ops["scalar"].append(s_sig)
                        C["e_sg"] += 1
                        sg_of[s_idx] = C["e_sg"]
                        if a_stage < 3:
                            continue

                        def v_sig(v, n=n, s_idx=s_idx, base_tq=base_tq, ready=C["e_sg"]):
                            v.wait_ge(S["e_sg"], ready)
                            if s_idx - 2 >= base_tq:
                                v.wait_ge(S["e_tn"], tn_of[s_idx - 2])
                            v.tensor_scalar(esb[:, :], esb[:, :], c_one[:, 0:1], None, ALU.min)
                            v.tensor_copy(c_junk[:, :], c_one[:, :])
                            v.tensor_tensor(sig_bf[n][:, :], rsb[:, :], esb[:, :],
                                            ALU.add).then_inc(S["e_vsig"], 1)
                        ops["vector"].append(v_sig)
                        C["e_vsig"] += 1
                        if a_stage < 4:
                            continue

                        def t_nd(t, m=m, n=n, s_idx=s_idx, base_tq=base_tq,
                                 vsig_done=C["e_vsig"]):
                            t.wait_ge(S["e_vsig"], vsig_done)
                            if s_idx - 1 >= base_tq:
                                t.wait_ge(S["e_vretr"], retr_of[s_idx - 1])
                            t.matmul(ps_n[n][:, :], m_bf[:, m * 128:(m + 1) * 128],
                                     sig_bf[n][:, :], start=True, stop=True)
                            t.matmul(ps_d[:, :], zrep[:, m * 128:(m + 1) * 128],
                                     sig_bf[n][:, :], start=True,
                                     stop=True).then_inc(S["e_tn"], 1)
                        ops["tensor"].append(t_nd)
                        C["e_tn"] += 1
                        tn_of[s_idx] = C["e_tn"]
                        if a_stage < 5:
                            continue

                        def v_retr(v, m=m, n=n, nd_done=C["e_tn"]):
                            v.wait_ge(S["e_tn"], nd_done)
                            v.tensor_scalar(dnf[:, :], ps_d[:, :], c_eps6[:, 0:1],
                                            None, ALU.add)
                            v.tensor_copy(c_junk[:, :], c_one[:, :])
                            v.reciprocal(rcp_f[:, :], dnf[:, :])
                            v.tensor_copy(c_junk[:, :], c_one[:, :])
                            v.tensor_copy(denb[:, :], rcp_f[:, :])
                            v.tensor_copy(c_junk[:, :], c_one[:, :])
                            v.tensor_tensor(
                                retrT[:, m * HALF + n * 512: m * HALF + (n + 1) * 512],
                                ps_n[n][:, :], denb[:, :], ALU.mult).then_inc(S["e_vretr"], 1)
                        ops["vector"].append(v_retr)
                        C["e_vretr"] += 1
                        retr_of[s_idx] = C["e_vretr"]

                emit(ops)
            barrier()
            if stop in ('ln0', 'a0') or stop.startswith('a1h'):
                continue

            with ExitStack() as esB:
                def sbB(name, shape, dt):
                    return esB.enter_context(nc.sbuf_tensor(f"{name}_h{half}", shape, dt))
                wdq = [sbB(f"wdq{i}", [128, KT * CW], BF16) for i in range(2)]
                w8b = [sbB(f"w8b{i}", [128, KT * CW], I8) for i in range(2)]
                scb = [sbB(f"scb{i}", [128, 4 * CW], F32) for i in range(4)]
                wobf = [sbB(f"wobf{i}", [128, H * CW], BF16) for i in range(2)]
                xrb = [sbB(f"xrb{i}", [128, NTILE * CW], F32) for i in range(2)]
                ost = [sbB(f"ost{i}", [128, NTILE * CW], F32) for i in range(2)]

                ops = mkops()
                ps_b4 = [ps_q[0], ps_q[1], ps_n[0], ps_n[1]]
                tf_base = C["e_tf"]
                ev_base = C["e_ev"]
                dqc_base = C["e_dqp"] // 8
                out_base = n_out
                xr_base = n_xr
                retr_ready = C["e_vretr"]

                def emit_evacs(v, c, half=half, ev_cnt_base=ev_base, tf_b=tf_base,
                               xr_b=xr_base, out_b=out_base):
                    cg = half * NCH + c
                    for tt in range(NTILE):
                        v.wait_ge(S["e_tf"], tf_b + c * NTILE + tt + 1)
                        if tt == 0:
                            v.wait_ge(S[f"sxr{c % 2}"], 16 * (cg // 2 + 1))
                            if c >= 2:
                                v.wait_ge(S[f"sout{c % 2}"], 16 * (cg // 2))
                        v.tensor_tensor(ost[c % 2][:, tt * CW:(tt + 1) * CW],
                                        ps_b4[tt % 4][:, 0:CW],
                                        xrb[c % 2][:, tt * CW:(tt + 1) * CW],
                                        ALU.add).then_inc(S["e_ev"], 1)

                for c in range(NCH):
                    cc0 = c * CW

                    def sy_w8(sy, c=c, cc0=cc0, dq_need=8 * (dqc_base + c - 1)):
                        if c >= 2:
                            sy.wait_ge(S["e_dqp"], dq_need)
                        sy.dma_start(
                            out=w8b[c % 2][:, :].rearrange("p (k j) -> p k j", k=KT),
                            in_=w8.rearrange("(k p) f -> p k f", p=128)[:, :, cc0:cc0 + CW],
                        ).then_inc(S[f"sw8{c % 2}"], 16)
                    ops["sync"].append(sy_w8)
                    n_w8 += 1

                    for kg in range(8):
                        def sy_sc(sy, c=c, cc0=cc0, kg=kg, pidx=n_sc):
                            if pidx >= 4:
                                sy.wait_ge(S["e_dqp"], pidx - 3)
                            sy.dma_start(
                                out=scb[pidx % 4][:, :].rearrange("p (kk j) -> p kk j", kk=4),
                                in_=wsc.rearrange("(kg kk p) f -> kg p kk f",
                                                  kg=8, p=128)[kg][:, :, cc0:cc0 + CW],
                            ).then_inc(S[f"ssc{pidx % 4}"], 16)
                        ops["sync"].append(sy_sc)
                        n_sc += 1

                    def sy_xr(sy, c=c, cc0=cc0, half=half, ev_need=ev_base + 8 * (c - 1)):
                        if c >= 2:
                            sy.wait_ge(S["e_ev"], ev_need)
                        sy.dma_start(
                            out=xrb[c % 2][:, :].rearrange("p (tt j) -> p tt j", tt=NTILE),
                            in_=x_in.rearrange("(hh tt p) f -> hh tt p f", hh=2, p=128)[half]
                                    .rearrange("tt p f -> p tt f")[:, :, cc0:cc0 + CW],
                        ).then_inc(S[f"sxr{c % 2}"], 16)
                    ops["sync"].append(sy_xr)
                    n_xr += 1

                    def sy_wot(sy, c=c, cc0=cc0, half=half, tf_need=tf_base + 8 * (c - 1)):
                        if half == 0 and c == 0:
                            sy.wait_ge(S["swos"], 16 * H)
                        if c >= 2:
                            sy.wait_ge(S["e_tf"], tf_need)
                        sy.dma_start(
                            out=wobf[c % 2][:, :].rearrange("p (g j) -> p g j", g=H),
                            in_=woT_s.rearrange("g p f -> p g f")[:, :, cc0:cc0 + CW],
                        ).then_inc(S[f"swot{c % 2}"], 16)
                    if not nowo:
                        ops["gpsimd"].append(sy_wot)

                    if c >= 1:
                        def g_out(g, cprev=c - 1, half=half, ev_need=ev_base + 8 * c,
                                  so=n_out):
                            g.wait_ge(S["e_ev"], ev_need)
                            if so >= 2:
                                g.wait_ge(S[f"sout{so % 2}"], 16 * (so // 2))
                            g.dma_start(
                                out=out.rearrange("(hh tt p) f -> hh tt p f", hh=2, p=128)[half]
                                       .rearrange("tt p f -> p tt f")[:, :, cprev * CW:(cprev + 1) * CW],
                                in_=ost[cprev % 2][:, :].rearrange("p (tt j) -> p tt j", tt=NTILE),
                            ).then_inc(S[f"sout{so % 2}"], 16)
                        ops["gpsimd"].append(g_out)
                        n_out += 1

                    def v_dq(v, c=c, w8_sem=f"sw8{c % 2}", w8_need=16 * ((half * NCH + c) // 2 + 1),
                             piece0=n_sc - 8, tf_need=tf_base + 8 * (c - 1)):
                        v.wait_ge(S[w8_sem], w8_need)
                        if c >= 2:
                            v.wait_ge(S["e_tf"], tf_need)
                        for k in range(KT):
                            kg = k // 4
                            if k % 4 == 0:
                                pp = piece0 + kg
                                v.wait_ge(S[f"ssc{pp % 4}"], 16 * (pp // 4 + 1))
                            pidx = piece0 + kg
                            ins = v.tensor_tensor(wdq[c % 2][:, k * CW:(k + 1) * CW],
                                                  w8b[c % 2][:, k * CW:(k + 1) * CW],
                                                  scb[pidx % 4][:, (k % 4) * CW:(k % 4 + 1) * CW],
                                                  ALU.mult)
                            if k % 4 == 3:
                                ins.then_inc(S["e_dqp"], 1)
                    ops["vector"].append(v_dq)
                    C["e_dqp"] += 8

                    if c >= 1:
                        def v_ev(v, cprev=c - 1):
                            emit_evacs(v, cprev)
                        ops["vector"].append(v_ev)
                        C["e_ev"] += NTILE

                    def t_ch(t, c=c, dq_need=8 * (dqc_base + c + 1),
                             wo_sem=f"swot{c % 2}", wo_need=16 * ((half * NCH + c) // 2 + 1),
                             ev_b=ev_base, retr_ready=retr_ready):
                        if c == 0:
                            t.wait_ge(S["e_vretr"], retr_ready)
                        t.wait_ge(S["e_dqp"], dq_need)
                        if not nowo:
                            t.wait_ge(S[wo_sem], wo_need)
                        for tt in range(NTILE):
                            gi = c * NTILE + tt
                            if gi >= 4:
                                t.wait_ge(S["e_ev"], ev_b + gi - 3)
                            psq = ps_b4[tt % 4]
                            for k in range(KT):
                                t.matmul(psq[:, 0:CW],
                                         xnT[:, k * HALF + tt * 128: k * HALF + (tt + 1) * 128],
                                         wdq[c % 2][:, k * CW:(k + 1) * CW],
                                         start=(k == 0), stop=False)
                            if not (nowo or nowomm):
                                for g in range(H):
                                    t.matmul(psq[:, 0:CW],
                                             retrT[:, g * HALF + tt * 128: g * HALF + (tt + 1) * 128],
                                             wobf[c % 2][:, g * CW:(g + 1) * CW],
                                             start=False, stop=False)
                            t.matmul(psq[:, 0:CW], ones_sq[:, :], b_rep[:, c * CW:(c + 1) * CW],
                                     start=False, stop=True).then_inc(S["e_tf"], 1)
                    ops["tensor"].append(t_ch)
                    C["e_tf"] += NTILE

                def v_ev_last(v, cprev=NCH - 1):
                    emit_evacs(v, cprev)
                ops["vector"].append(v_ev_last)
                C["e_ev"] += NTILE

                def g_out_last(g, cprev=NCH - 1, half=half, ev_need=ev_base + 8 * NCH,
                               so=n_out):
                    g.wait_ge(S["e_ev"], ev_need)
                    if so >= 2:
                        g.wait_ge(S[f"sout{so % 2}"], 16 * (so // 2))
                    g.dma_start(
                        out=out.rearrange("(hh tt p) f -> hh tt p f", hh=2, p=128)[half]
                               .rearrange("tt p f -> p tt f")[:, :, cprev * CW:(cprev + 1) * CW],
                        in_=ost[cprev % 2][:, :].rearrange("p (tt j) -> p tt j", tt=NTILE),
                    ).then_inc(S[f"sout{so % 2}"], 16)
                ops["gpsimd"].append(g_out_last)
                n_out += 1

                emit(ops)
            barrier()

        if n_out:
            ops = mkops()
            t0v = 16 * ((n_out + 1) // 2)
            t1v = 16 * (n_out // 2)
            def g_fin(g, t0v=t0v, t1v=t1v):
                g.wait_ge(S["sout0"], t0v)
                g.wait_ge(S["sout1"], t1v)
            ops["gpsimd"].append(g_fin)
            emit(ops)

    return nc


_NC_CACHE = {}


def kernel(L0_attn_hidden_states, L0_attn_Wq, L0_attn_bq, L0_attn_Wk, L0_attn_bk,
           L0_attn_Wv, L0_attn_bv, L0_attn_Wo, L0_attn_M, L0_attn_z,
           L0_ffn_w, L0_ffn_w_scale, L0_ffn_b):
    x = np.ascontiguousarray(np.asarray(L0_attn_hidden_states, dtype=np.float32))
    Wq = np.asarray(L0_attn_Wq, dtype=np.float32)
    bq = np.asarray(L0_attn_bq, dtype=np.float32)
    Wo = np.asarray(L0_attn_Wo, dtype=np.float32)
    M = np.ascontiguousarray(np.asarray(L0_attn_M, dtype=np.float32))
    z = np.ascontiguousarray(np.asarray(L0_attn_z, dtype=np.float32))
    w8 = np.ascontiguousarray(np.asarray(L0_ffn_w, dtype=np.int8))
    wsc = np.ascontiguousarray(np.asarray(L0_ffn_w_scale, dtype=np.float32))
    bff = np.ascontiguousarray(np.asarray(L0_ffn_b, dtype=np.float32).reshape(1, D))

    Wq_sel = Wq.reshape(8, 4, 128, D)[:, 0].reshape(8 * 128, D)
    wqT = np.ascontiguousarray(Wq_sel.T)
    bq2 = np.ascontiguousarray(bq.reshape(8, 4, 128)[:, 0])
    woTb = np.ascontiguousarray(
        Wo.T.reshape(8, 4, 128, D).transpose(1, 0, 2, 3).reshape(4, 1024, D))
    ident = np.eye(128, dtype=np.float32)

    if "nc" not in _NC_CACHE:
        _NC_CACHE["nc"] = build_program()
    nc = _NC_CACHE["nc"]

    in_maps = []
    for i in range(8):
        in_maps.append(dict(x_in=np.ascontiguousarray(x[i]), wqT=wqT, bq2=bq2,
                            m_in=M, z_in=z, woTb=woTb, w8=w8, wsc=wsc, bff=bff,
                            ident=ident))
    res = run_bass_kernel_spmd(nc, in_maps, core_ids=list(range(8)))
    return np.stack([r["out"] for r in res.results], axis=0)


# version 15 (final)
# speedup vs baseline: 1.3570x; 1.0001x over previous
import numpy as np
from contextlib import ExitStack

import concourse.bass as bass
import concourse.mybir as mybir
from concourse.bass_utils import run_bass_kernel_spmd

F32 = mybir.dt.float32
BF16 = mybir.dt.bfloat16
I8 = mybir.dt.int8
AF = mybir.ActivationFunctionType
ALU = mybir.AluOpType
AXX = mybir.AxisListType.X

D = 4096
KT = 32
H = 8
CW = 256
NCH = D // CW
TOK = 2048
HALF = TOK // 2
NTILE = HALF // 128
NSEQ = HALF // 512

ENGS = ("sync", "gpsimd", "scalar", "vector", "tensor")


def build_program(stop='full', nowo=False, nowomm=False):
    nc = bass.Bass("TRN2", target_bir_lowering=False)

    x_in = nc.dram_tensor("x_in", [TOK, D], F32, kind="ExternalInput")
    wqT = nc.dram_tensor("wqT", [D, H * 128], F32, kind="ExternalInput")
    bq2 = nc.dram_tensor("bq2", [H, 128], F32, kind="ExternalInput")
    m_in = nc.dram_tensor("m_in", [H, 128, 128], F32, kind="ExternalInput")
    z_in = nc.dram_tensor("z_in", [H, 128], F32, kind="ExternalInput")
    woTb = nc.dram_tensor("woTb", [4, H * 128, D], F32, kind="ExternalInput")
    w8 = nc.dram_tensor("w8", [D, D], I8, kind="ExternalInput")
    wsc = nc.dram_tensor("wsc", [D, D], F32, kind="ExternalInput")
    bff = nc.dram_tensor("bff", [1, D], F32, kind="ExternalInput")
    ident = nc.dram_tensor("ident", [128, 128], F32, kind="ExternalInput")
    out = nc.dram_tensor("out", [TOK, D], F32, kind="ExternalOutput")
    woT_s = nc.dram_tensor("woT_s", [H, 128, D], BF16, kind="Internal")
    wqT_s = nc.dram_tensor("wqT_s", [KT, 128, H * 128], BF16, kind="Internal")

    with ExitStack() as es:
        def sbuf(name, shape, dt):
            return es.enter_context(nc.sbuf_tensor(name, shape, dt))

        xnT = sbuf("xnT", [128, KT * HALF], BF16)
        retrT = sbuf("retrT", [128, H * HALF], BF16)
        m_bf = sbuf("m_bf", [128, H * 128], BF16)
        zrep = sbuf("zrep", [128, H * 128], BF16)
        b_rep = sbuf("b_rep", [128, D], BF16)
        ident_bf = sbuf("ident_bf", [128, 128], BF16)
        ones_sq = sbuf("ones_sq", [128, 128], BF16)
        onesf = sbuf("onesf", [128, 128], F32)
        z_g = [sbuf(f"z_g{g}", [128, 1], F32) for g in range(H)]
        bq_g = [sbuf(f"bq_g{g}", [128, 1], F32) for g in range(H)]
        c_eps5 = sbuf("c_eps5", [128, 1], F32)
        c_eps6 = sbuf("c_eps6", [128, 1], F32)
        c_invd = sbuf("c_invd", [128, 1], F32)
        c_neg1 = sbuf("c_neg1", [128, 1], F32)
        c_one = sbuf("c_one", [128, 1], F32)
        c_i128 = sbuf("c_i128", [128, 1], F32)
        c_junk = sbuf("c_junk", [128, 1], F32)
        c_nsum = sbuf("c_nsum", [128, 1], F32)
        c_nm = sbuf("c_nm", [128, 1], F32)
        c_ssq = sbuf("c_ssq", [128, 1], F32)
        c_nm2n = sbuf("c_nm2n", [128, 1], F32)
        c_var = sbuf("c_var", [128, 1], F32)
        c_sig = sbuf("c_sig", [128, 1], F32)
        c_inv = sbuf("c_inv", [128, 1], F32)

        ps_q = [es.enter_context(nc.psum_tensor(f"ps_q{i}", [128, 512], F32)) for i in range(2)]
        ps_n = [es.enter_context(nc.psum_tensor(f"ps_n{i}", [128, 512], F32)) for i in range(2)]
        ps_b = es.enter_context(nc.psum_tensor("ps_b", [128, 512], F32))
        ps_d = es.enter_context(nc.psum_tensor("ps_d", [128, 512], F32))
        ps_tr = [es.enter_context(nc.psum_tensor(f"ps_tr{i}", [128, 128], F32)) for i in range(2)]

        sem_names = ["sxt0", "sxt1", "swq0", "swq1", "sw80", "sw81", "ssc0", "ssc1", "ssc2", "ssc3",
                     "sxr0", "sxr1", "sout0", "sout1", "sacc0", "sacc1", "sacc2", "sacc3", "swos",
                     "swot0", "swot1", "scst", "swqs", "swqc",
                     "e_g", "e_ln", "e_lnr", "e_vtr", "e_ttr",
                     "e_tq", "e_sg", "e_vsig", "e_tn", "e_vr", "e_tb", "e_vretr",
                     "e_dqc", "e_dqp", "e_vwoc", "e_ev", "e_tf", "e_ch", "ph"]
        S = {n: es.enter_context(nc.semaphore(n)) for n in sem_names}
        C = {n: 0 for n in sem_names}

        n_xt = n_wq = n_w8 = n_sc = n_xr = n_out = n_acc = 0

        block = es.enter_context(nc.Block())

        def emit(ops):
            for eng in ENGS:
                lst = ops.get(eng)
                if not lst:
                    continue
                getattr(block, eng)(lambda h, lst=lst: [f(h) for f in lst])

        def mkops():
            return {e: [] for e in ENGS}

        def barrier():
            ops = mkops()
            C["ph"] += 5
            tgt = C["ph"]
            pre = {
                "tensor": [("e_vretr", C["e_vretr"]), ("e_ev", C["e_ev"]),
                           ("e_vtr", C["e_vtr"]), ("e_vsig", C["e_vsig"])],
                "gpsimd": [("sout0", 16 * ((n_out + 1) // 2)), ("sout1", 16 * (n_out // 2)),
                           ("sacc0", C["sacc0"]), ("sacc1", C["sacc1"]),
                           ("sacc2", C["sacc2"]), ("sacc3", C["sacc3"]), ("swos", C["swos"])],
                "vector": [("e_tf", C["e_tf"])],
                "scalar": [],
                "sync": [],
            }
            for eng in ENGS:
                def f(h, t=tgt, waits=tuple(pre[eng])):
                    for sname, val in waits:
                        if val:
                            h.wait_ge(S[sname], val)
                    h.nop().then_inc(S["ph"], 1)
                    h.wait_ge(S["ph"], t)
                ops[eng].append(f)
            emit(ops)

        sg_of, tn_of, vr_of, retr_of = {}, {}, {}, {}

        if stop == 'bar':
            barrier()
            return nc
        es_pro = ExitStack()
        bcol = [es_pro.enter_context(nc.sbuf_tensor(f"bcol{j}", [128, 1], F32))
                for j in range(KT)]
        rep_f = es_pro.enter_context(nc.sbuf_tensor("rep_f", [128, 128], F32))
        rep_b = es_pro.enter_context(nc.sbuf_tensor("rep_b", [128, 128], BF16))
        ops = mkops()

        def g_pro(g):
            g.memset(ones_sq[:, :], 1.0)
            g.memset(onesf[:, :], 1.0)
            g.memset(c_eps5[:, :], 1e-5)
            g.memset(c_eps6[:, :], 1e-6)
            g.memset(c_invd[:, :], 1.0 / D)
            g.memset(c_neg1[:, :], -1.0)
            g.memset(c_one[:, :], 1.0)
            g.memset(c_i128[:, :], 1.0 / 128.0)
            g.memset(c_junk[:, :], 0.0).then_inc(S["e_g"], 1)
            g.dma_start(out=m_bf[:, :].rearrange("p (g e) -> p g e", g=H),
                        in_=m_in.rearrange("g d e -> d g e")).then_inc(S["scst"], 16)
            g.dma_start(out=ident_bf[:, :], in_=ident[:, :]).then_inc(S["scst"], 16)
            for j in range(KT):
                g.dma_start(out=bcol[j][:, :],
                            in_=bff.rearrange("o f -> f o")[j * 128:(j + 1) * 128, :],
                            allow_slow_non_contiguous=True).then_inc(S["scst"], 16)
            for gg in range(H):
                g.dma_start(out=z_g[gg][:, :], in_=z_in.rearrange("g d -> d g")[:, gg:gg + 1],
                            allow_slow_non_contiguous=True).then_inc(S["scst"], 16)
                g.dma_start(out=bq_g[gg][:, :], in_=bq2.rearrange("g d -> d g")[:, gg:gg + 1],
                            allow_slow_non_contiguous=True).then_inc(S["scst"], 16)
        ops["gpsimd"].append(g_pro)
        C["e_g"] += 1
        C["scst"] += 16 * (2 + KT + 2 * H)
        cst_done = C["scst"]
        if stop == 'pro_g':
            emit(ops)
            es_pro.close()
            return nc

        def s_pro(s, cst_done=cst_done):
            s.wait_ge(S["e_g"], 1)
            s.wait_ge(S["scst"], cst_done)
            ins = None
            for gg in range(H):
                ins = s.activation(zrep[:, gg * 128:(gg + 1) * 128], onesf[:, :], AF.Copy,
                                   scale=z_g[gg][:, 0:1])
            ins.then_inc(S["e_sg"], 1)
        ops["scalar"].append(s_pro)
        C["e_sg"] += 1
        if stop == 'pro_s':
            emit(ops)
            es_pro.close()
            return nc

        def v_pro(v, cst_done=cst_done):
            v.wait_ge(S["e_g"], 1)
            v.wait_ge(S["scst"], cst_done)
            for j in range(KT):
                v.tensor_scalar(bcol[j][:, :], bcol[j][:, :], c_i128[:, 0:1],
                                None, ALU.mult).then_inc(S["e_ln"], 1)
        ops["vector"].append(v_pro)
        C["e_ln"] += KT
        if stop == 'pro_v':
            emit(ops)
            es_pro.close()
            return nc

        for j in range(KT):
            def s_bj(s, j=j, v_need=C["e_vsig"] + 0, ln_need=C["e_ln"] - KT + j + 1):
                s.wait_ge(S["e_ln"], ln_need)
                if j != 0:
                    s.wait_ge(S["e_vsig"], v_need)
                s.activation(rep_f[:, :], onesf[:, :], AF.Copy,
                             scale=bcol[j][:, 0:1]).then_inc(S["e_sg"], 1)
            ops["scalar"].append(s_bj)
            C["e_sg"] += 1

            def v_bj(v, j=j, s_need=C["e_sg"], t_need=C["e_tb"]):
                v.wait_ge(S["e_sg"], s_need)
                if j >= 1:
                    v.wait_ge(S["e_tb"], t_need)
                v.tensor_copy(rep_b[:, :], rep_f[:, :]).then_inc(S["e_vsig"], 1)
            ops["vector"].append(v_bj)
            C["e_vsig"] += 1

            def t_bj(t, j=j, v_need=C["e_vsig"], e_need=C["e_vretr"]):
                if j == 0:
                    t.wait_ge(S["e_g"], 1)
                t.wait_ge(S["e_vsig"], v_need)
                if j >= 1:
                    t.wait_ge(S["e_vretr"], e_need)
                t.matmul(ps_b[:, 0:128], rep_b[:, :], ident_bf[:, :],
                         start=True, stop=True).then_inc(S["e_tb"], 1)
            ops["tensor"].append(t_bj)
            C["e_tb"] += 1

            def v_bj2(v, j=j, t_need=C["e_tb"]):
                v.wait_ge(S["e_tb"], t_need)
                v.tensor_copy(b_rep[:, j * 128:(j + 1) * 128], ps_b[:, 0:128]).then_inc(S["e_vretr"], 1)
            ops["vector"].append(v_bj2)
            C["e_vretr"] += 1

        emit(ops)
        es_pro.close()
        barrier()

        halves = [] if stop == 'pro' else ([0] if stop in ('ln0', 'a0', 'b0') else [0, 1])
        for half in halves:
            with ExitStack() as esA:
                def sbA(name, shape, dt):
                    return esA.enter_context(nc.sbuf_tensor(f"{name}_h{half}", shape, dt))
                x_t = [sbA(f"x_t{i}", [128, D], F32) for i in range(2)]
                xsq = sbA("xsq", [128, D], BF16)
                xn = [sbA(f"xn{i}", [128, D], BF16) for i in range(2)]
                wq_b = [sbA(f"wq_b{i}", [128, KT * 128], BF16) for i in range(2)]
                esb = sbA("esb", [128, 512], F32)
                rsb = sbA("rsb", [128, 512], F32)
                sig_bf = [sbA(f"sig_bf{i}", [128, 512], BF16) for i in range(2)]
                dnf = sbA("dnf", [128, 512], F32)
                if half == 0:
                    woraw = sbA("woraw", [128, D], F32)
                    wo_stage = sbA("wo_stage", [128, D], BF16)
                rcp_f = sbA("rcp_f", [128, 512], F32)
                denb = sbA("denb", [128, 512], BF16)

                ops = mkops()
                tr_ps4 = [ps_tr[0], ps_tr[1], ps_b, ps_d]
                ln_base = C["e_ln"]
                lnr_base = C["e_lnr"]
                vtr_base = C["e_vtr"]
                ttr_base = C["e_ttr"]

                for tloc in range(NTILE):
                    T = half * NTILE + tloc

                    def f_x(sy, T=T, tloc=tloc, need=ln_base + tloc - 1):
                        if tloc >= 2:
                            sy.wait_ge(S["e_ln"], need)
                        sy.dma_start(out=x_t[tloc % 2][:, :],
                                     in_=x_in[T * 128:(T + 1) * 128, :]).then_inc(S[f"sxt{T % 2}"], 16)
                    ops["sync"].append(f_x)
                    n_xt += 1

                for tloc in range(NTILE):
                    Tg = half * NTILE + tloc
                    xt_sem = f"sxt{Tg % 2}"
                    xt_need = 16 * (Tg // 2 + 1)

                    def s_sq(s, tloc=tloc, xt_sem=xt_sem, xt_need=xt_need, lnr_need=lnr_base + tloc):
                        s.wait_ge(S[xt_sem], xt_need)
                        if tloc >= 1:
                            s.wait_ge(S["e_lnr"], lnr_need)
                        s.activation(xsq[:, :], x_t[tloc % 2][:, :],
                                     AF.Square).then_inc(S["e_sg"], 1)
                    ops["scalar"].append(s_sq)
                    C["e_sg"] += 1
                    sq_done = C["e_sg"]

                    def v_st(v, tloc=tloc, xt_sem=xt_sem, xt_need=xt_need, sq_done=sq_done,
                             ch0=C["e_ch"], lnr_c=C["e_lnr"] + 1):
                        v.wait_ge(S[xt_sem], xt_need)
                        xt = x_t[tloc % 2]
                        v.tensor_reduce(c_nsum[:, :], xt[:, :], AXX, ALU.add,
                                        negate=True).then_inc(S["e_ch"], 1)
                        v.wait_ge(S["e_ch"], ch0 + 1)
                        v.tensor_scalar(c_nm[:, :], c_nsum[:, :], c_invd[:, 0:1],
                                        None, ALU.mult).then_inc(S["e_ch"], 1)
                        v.wait_ge(S["e_sg"], sq_done)
                        v.tensor_reduce(c_ssq[:, :], xsq[:, :], AXX, ALU.add).then_inc(S["e_lnr"], 1)
                        v.wait_ge(S["e_ch"], ch0 + 2)
                        v.tensor_scalar(c_nm2n[:, :], c_nm[:, :], c_nm[:, 0:1], c_neg1[:, 0:1],
                                        ALU.mult, ALU.mult).then_inc(S["e_ch"], 1)
                        v.wait_ge(S["e_lnr"], lnr_c)
                        v.tensor_scalar(c_ssq[:, :], c_ssq[:, :], c_invd[:, 0:1],
                                        None, ALU.mult).then_inc(S["e_ch"], 1)
                        v.wait_ge(S["e_ch"], ch0 + 4)
                        v.tensor_scalar(c_var[:, :], c_ssq[:, :], c_nm2n[:, 0:1],
                                        None, ALU.add).then_inc(S["e_vr"], 1)
                    ops["vector"].append(v_st)
                    C["e_ch"] += 4
                    C["e_lnr"] += 1
                    C["e_vr"] += 1
                    var_done = C["e_vr"]

                    def s_rt(s, var_done=var_done):
                        s.wait_ge(S["e_vr"], var_done)
                        s.activation(c_sig[:, :], c_var[:, :], AF.Sqrt,
                                     bias=c_eps5[:, 0:1]).then_inc(S["e_sg"], 1)
                    ops["scalar"].append(s_rt)
                    C["e_sg"] += 1
                    sig_done = C["e_sg"]

                    def v_xn(v, tloc=tloc, sig_done=sig_done,
                             ttr_need=ttr_base + 32 * (tloc - 1), ch1=C["e_ch"] + 1):
                        v.wait_ge(S["e_sg"], sig_done)
                        v.reciprocal(c_inv[:, :], c_sig[:, :]).then_inc(S["e_ch"], 1)
                        if tloc >= 2:
                            v.wait_ge(S["e_ttr"], ttr_need)
                        v.wait_ge(S["e_ch"], ch1)
                        v.tensor_scalar(xn[tloc % 2][:, :], x_t[tloc % 2][:, :],
                                        c_nm[:, 0:1], c_inv[:, 0:1],
                                        ALU.add, ALU.mult).then_inc(S["e_ln"], 1)
                    ops["vector"].append(v_xn)
                    C["e_ch"] += 1
                    C["e_ln"] += 1
                    xn_done = C["e_ln"]

                    def t_tr(t, tloc=tloc, xn_done=xn_done, vtr_b=vtr_base + 32 * tloc,
                             first=(half == 0 and tloc == 0)):
                        t.wait_ge(S["e_ln"], xn_done)
                        for k in range(KT):
                            gk = vtr_b - vtr_base + k
                            if gk >= 4:
                                t.wait_ge(S["e_vtr"], vtr_b + k - 3)
                            t.matmul(tr_ps4[k % 4][:, 0:128],
                                     xn[tloc % 2][:, k * 128:(k + 1) * 128],
                                     ident_bf[:, :], start=True,
                                     stop=True).then_inc(S["e_ttr"], 1)
                    ops["tensor"].append(t_tr)
                    C["e_ttr"] += KT

                    def v_tr(v, tloc=tloc, ttr_b=ttr_base + 32 * tloc):
                        for k in range(KT):
                            v.wait_ge(S["e_ttr"], ttr_b + k + 1)
                            v.tensor_copy(
                                xnT[:, k * HALF + tloc * 128: k * HALF + (tloc + 1) * 128],
                                tr_ps4[k % 4][:, 0:128]).then_inc(S["e_vtr"], 1)
                    ops["vector"].append(v_tr)
                    C["e_vtr"] += KT

                all_tr = C["e_vtr"]
                base_tq = C["e_tq"]
                do_attn = stop not in ('ln0',)
                a_stage = 5
                if stop.startswith('a1h'):
                    a_stage = int(stop[4:])

                mlist = list(range(H)) if do_attn else []
                if stop.startswith('a1h'):
                    mlist = [0]
                wq_cnt_of = {}

                def emit_wq(m):
                    def f_wq(sy, m=m, tq_need=base_tq + 2 * (m - 1)):
                        if m >= 2:
                            sy.wait_ge(S["e_tq"], tq_need)
                        else:
                            sy.wait_ge(S["swqs"], 128)
                        sy.dma_start(
                            out=wq_b[m % 2][:, :].rearrange("p (k q) -> p k q", k=KT),
                            in_=wqT_s.rearrange("k p q -> p k q")[:, :, m * 128:(m + 1) * 128],
                        ).then_inc(S[f"swq{m % 2}"], 16)
                    ops["sync"].append(f_wq)

                def emit_wo_transform(g_idx):
                    for r in range(4):
                        for s_q in range(4):
                            sem = f"sacc{s_q}"
                            def f_acc(g, g_idx=g_idx, r=r, s_q=s_q, sem=sem,
                                      prev=C[sem]):
                                if prev:
                                    g.wait_ge(S[sem], prev)
                                if r == 0 and g_idx:
                                    g.wait_ge(S["swos"], 16 * g_idx)
                                cols = slice(s_q * (D // 4), (s_q + 1) * (D // 4))
                                srcw = woTb.rearrange("r (g p) f -> r p g f", p=128)[r][:, g_idx, cols]
                                if r == 0:
                                    g.dma_start(out=woraw[:, cols], in_=srcw).then_inc(S[sem], 16)
                                else:
                                    g.dma_start(out=woraw[:, cols], in_=srcw,
                                                accum_op=ALU.add).then_inc(S[sem], 16)
                            ops["gpsimd"].append(f_acc)
                            C[sem] += 16

                    def f_cast(g, targs=tuple((f"sacc{i}", C[f"sacc{i}"]) for i in range(4)),
                               first=(g_idx == 0)):
                        if first:
                            g.wait_ge(S["swqs"], 128)
                        for sname, val in targs:
                            g.wait_ge(S[sname], val)
                        g.tensor_copy(wo_stage[:, :], woraw[:, :]).then_inc(S["e_g"], 1)
                    ops["gpsimd"].append(f_cast)
                    C["e_g"] += 1

                    def f_wos(g, g_idx=g_idx, gc=C["e_g"]):
                        g.wait_ge(S["e_g"], gc)
                        g.dma_start(out=woT_s[g_idx], in_=wo_stage[:, :]).then_inc(S["swos"], 16)
                    ops["gpsimd"].append(f_wos)
                    C["swos"] += 16

                if mlist:
                    if half == 0:
                        def f_strips(g):
                            for s in range(8):
                                if s >= 1:
                                    g.wait_ge(S["swqs"], 16 * s)
                                g.dma_start(
                                    out=wo_stage[:, :].rearrange("p (k q) -> p k q", k=4),
                                    in_=wqT.rearrange("(s k p) q -> s p k q",
                                                      s=8, p=128)[s],
                                ).then_inc(S["swqc"], 16)
                                g.wait_ge(S["swqc"], 16 * (s + 1))
                                g.dma_start(
                                    out=wqT_s[4 * s:4 * s + 4].rearrange("k p q -> p k q"),
                                    in_=wo_stage[:, :].rearrange("p (k q) -> p k q", k=4),
                                ).then_inc(S["swqs"], 16)
                        ops["gpsimd"].append(f_strips)
                        C["swqc"] += 128
                        C["swqs"] += 128
                        for g_idx in range(H):
                            emit_wo_transform(g_idx)
                    for m in mlist:
                        emit_wq(m)
                    for m in mlist:
                        n_wq += 1
                        wq_cnt_of[m] = (f"swq{m % 2}", 16 * ((half * H + m) // 2 + 1))

                for m in mlist:
                    for n in (range(NSEQ) if not stop.startswith('a1h') else [0]):
                        s_idx = C["e_tq"]

                        def t_q(t, m=m, n=n, s_idx=s_idx, all_tr=all_tr, base_tq=base_tq,
                                wq_need=wq_cnt_of[m]):
                            if m == 0 and n == 0:
                                t.wait_ge(S["e_vtr"], all_tr)
                            t.wait_ge(S[wq_need[0]], wq_need[1])
                            if s_idx - 2 >= base_tq:
                                t.wait_ge(S["e_sg"], sg_of[s_idx - 2])
                            ins = None
                            for k in range(KT):
                                ins = t.matmul(ps_q[n][:, :], wq_b[m % 2][:, k * 128:(k + 1) * 128],
                                               xnT[:, k * HALF + n * 512: k * HALF + (n + 1) * 512],
                                               start=(k == 0), stop=(k == KT - 1))
                            ins.then_inc(S["e_tq"], 1)
                        ops["tensor"].append(t_q)
                        C["e_tq"] += 1
                        if a_stage < 2:
                            continue
                        tq_done = C["e_tq"]

                        def s_sig(s, m=m, n=n, tq_done=tq_done):
                            s.wait_ge(S["e_tq"], tq_done)
                            s.activation(esb[:, :], ps_q[n][:, :], AF.Exp, bias=bq_g[m][:, 0:1])
                            s.activation(rsb[:, :], ps_q[n][:, :], AF.Relu,
                                         bias=bq_g[m][:, 0:1]).then_inc(S["e_sg"], 1)
                        ops["scalar"].append(s_sig)
                        C["e_sg"] += 1
                        sg_of[s_idx] = C["e_sg"]
                        if a_stage < 3:
                            continue

                        def v_sig(v, n=n, s_idx=s_idx, base_tq=base_tq, ready=C["e_sg"],
                                  ch2=C["e_ch"] + 1):
                            v.wait_ge(S["e_sg"], ready)
                            if s_idx - 2 >= base_tq:
                                v.wait_ge(S["e_tn"], tn_of[s_idx - 2])
                            v.tensor_scalar(esb[:, :], esb[:, :], c_one[:, 0:1],
                                            None, ALU.min).then_inc(S["e_ch"], 1)
                            v.wait_ge(S["e_ch"], ch2)
                            v.tensor_tensor(sig_bf[n][:, :], rsb[:, :], esb[:, :],
                                            ALU.add).then_inc(S["e_vsig"], 1)
                        ops["vector"].append(v_sig)
                        C["e_ch"] += 1
                        C["e_vsig"] += 1
                        if a_stage < 4:
                            continue

                        def t_nd(t, m=m, n=n, s_idx=s_idx, base_tq=base_tq,
                                 vsig_done=C["e_vsig"]):
                            t.wait_ge(S["e_vsig"], vsig_done)
                            if s_idx - 1 >= base_tq:
                                t.wait_ge(S["e_vretr"], retr_of[s_idx - 1])
                            t.matmul(ps_n[n][:, :], m_bf[:, m * 128:(m + 1) * 128],
                                     sig_bf[n][:, :], start=True, stop=True)
                            t.matmul(ps_d[:, :], zrep[:, m * 128:(m + 1) * 128],
                                     sig_bf[n][:, :], start=True,
                                     stop=True).then_inc(S["e_tn"], 1)
                        ops["tensor"].append(t_nd)
                        C["e_tn"] += 1
                        tn_of[s_idx] = C["e_tn"]
                        if a_stage < 5:
                            continue

                        def v_retr(v, m=m, n=n, nd_done=C["e_tn"], ch3=C["e_ch"]):
                            v.wait_ge(S["e_tn"], nd_done)
                            v.tensor_scalar(dnf[:, :], ps_d[:, :], c_eps6[:, 0:1],
                                            None, ALU.add).then_inc(S["e_ch"], 1)
                            v.wait_ge(S["e_ch"], ch3 + 1)
                            v.reciprocal(rcp_f[:, :], dnf[:, :]).then_inc(S["e_ch"], 1)
                            v.wait_ge(S["e_ch"], ch3 + 2)
                            v.tensor_copy(denb[:, :], rcp_f[:, :]).then_inc(S["e_ch"], 1)
                            v.wait_ge(S["e_ch"], ch3 + 3)
                            v.tensor_tensor(
                                retrT[:, m * HALF + n * 512: m * HALF + (n + 1) * 512],
                                ps_n[n][:, :], denb[:, :], ALU.mult).then_inc(S["e_vretr"], 1)
                        ops["vector"].append(v_retr)
                        C["e_ch"] += 3
                        C["e_vretr"] += 1
                        retr_of[s_idx] = C["e_vretr"]

                emit(ops)
            barrier()
            if stop in ('ln0', 'a0') or stop.startswith('a1h'):
                continue

            with ExitStack() as esB:
                def sbB(name, shape, dt):
                    return esB.enter_context(nc.sbuf_tensor(f"{name}_h{half}", shape, dt))
                wdq = [sbB(f"wdq{i}", [128, KT * CW], BF16) for i in range(2)]
                w8b = [sbB(f"w8b{i}", [128, KT * CW], I8) for i in range(2)]
                scb = [sbB(f"scb{i}", [128, 4 * CW], F32) for i in range(4)]
                wobf = [sbB(f"wobf{i}", [128, H * CW], BF16) for i in range(2)]
                xrb = [sbB(f"xrb{i}", [128, NTILE * CW], F32) for i in range(2)]
                ost = [sbB(f"ost{i}", [128, NTILE * CW], F32) for i in range(2)]

                ops = mkops()
                ps_b4 = [ps_q[0], ps_q[1], ps_n[0], ps_n[1]]
                tf_base = C["e_tf"]
                ev_base = C["e_ev"]
                dqc_base = C["e_dqp"] // 8
                out_base = n_out
                xr_base = n_xr
                retr_ready = C["e_vretr"]

                def emit_evacs(v, c, half=half, ev_cnt_base=ev_base, tf_b=tf_base,
                               xr_b=xr_base, out_b=out_base):
                    cg = half * NCH + c
                    for tt in range(NTILE):
                        v.wait_ge(S["e_tf"], tf_b + c * NTILE + tt + 1)
                        if tt == 0:
                            v.wait_ge(S[f"sxr{c % 2}"], 16 * (cg // 2 + 1))
                            if c >= 2:
                                v.wait_ge(S[f"sout{c % 2}"], 16 * (cg // 2))
                        v.tensor_tensor(ost[c % 2][:, tt * CW:(tt + 1) * CW],
                                        ps_b4[tt % 4][:, 0:CW],
                                        xrb[c % 2][:, tt * CW:(tt + 1) * CW],
                                        ALU.add).then_inc(S["e_ev"], 1)

                for c in range(NCH):
                    cc0 = c * CW

                    def sy_w8(sy, c=c, cc0=cc0, dq_need=8 * (dqc_base + c - 1)):
                        if c >= 2:
                            sy.wait_ge(S["e_dqp"], dq_need)
                        sy.dma_start(
                            out=w8b[c % 2][:, :].rearrange("p (k j) -> p k j", k=KT),
                            in_=w8.rearrange("(k p) f -> p k f", p=128)[:, :, cc0:cc0 + CW],
                        ).then_inc(S[f"sw8{c % 2}"], 16)
                    ops["sync"].append(sy_w8)
                    n_w8 += 1

                    for kg in range(8):
                        def sy_sc(sy, c=c, cc0=cc0, kg=kg, pidx=n_sc):
                            if pidx >= 4:
                                sy.wait_ge(S["e_dqp"], pidx - 3)
                            sy.dma_start(
                                out=scb[pidx % 4][:, :].rearrange("p (kk j) -> p kk j", kk=4),
                                in_=wsc.rearrange("(kg kk p) f -> kg p kk f",
                                                  kg=8, p=128)[kg][:, :, cc0:cc0 + CW],
                            ).then_inc(S[f"ssc{pidx % 4}"], 16)
                        ops["sync"].append(sy_sc)
                        n_sc += 1

                    def sy_xr(sy, c=c, cc0=cc0, half=half, ev_need=ev_base + 8 * (c - 1)):
                        if c >= 2:
                            sy.wait_ge(S["e_ev"], ev_need)
                        sy.dma_start(
                            out=xrb[c % 2][:, :].rearrange("p (tt j) -> p tt j", tt=NTILE),
                            in_=x_in.rearrange("(hh tt p) f -> hh tt p f", hh=2, p=128)[half]
                                    .rearrange("tt p f -> p tt f")[:, :, cc0:cc0 + CW],
                        ).then_inc(S[f"sxr{c % 2}"], 16)
                    ops["sync"].append(sy_xr)
                    n_xr += 1

                    def sy_wot(sy, c=c, cc0=cc0, half=half, tf_need=tf_base + 8 * (c - 1)):
                        if half == 0 and c == 0:
                            sy.wait_ge(S["swos"], 16 * H)
                        if c >= 2:
                            sy.wait_ge(S["e_tf"], tf_need)
                        sy.dma_start(
                            out=wobf[c % 2][:, :].rearrange("p (g j) -> p g j", g=H),
                            in_=woT_s.rearrange("g p f -> p g f")[:, :, cc0:cc0 + CW],
                        ).then_inc(S[f"swot{c % 2}"], 16)
                    if not nowo:
                        ops["gpsimd"].append(sy_wot)

                    if c >= 1:
                        def g_out(g, cprev=c - 1, half=half, ev_need=ev_base + 8 * c,
                                  so=n_out):
                            g.wait_ge(S["e_ev"], ev_need)
                            if so >= 2:
                                g.wait_ge(S[f"sout{so % 2}"], 16 * (so // 2))
                            g.dma_start(
                                out=out.rearrange("(hh tt p) f -> hh tt p f", hh=2, p=128)[half]
                                       .rearrange("tt p f -> p tt f")[:, :, cprev * CW:(cprev + 1) * CW],
                                in_=ost[cprev % 2][:, :].rearrange("p (tt j) -> p tt j", tt=NTILE),
                            ).then_inc(S[f"sout{so % 2}"], 16)
                        ops["gpsimd"].append(g_out)
                        n_out += 1

                    def v_dq(v, c=c, w8_sem=f"sw8{c % 2}", w8_need=16 * ((half * NCH + c) // 2 + 1),
                             piece0=n_sc - 8, tf_need=tf_base + 8 * (c - 1)):
                        v.wait_ge(S[w8_sem], w8_need)
                        if c >= 2:
                            v.wait_ge(S["e_tf"], tf_need)
                        for k in range(KT):
                            kg = k // 4
                            if k % 4 == 0:
                                pp = piece0 + kg
                                v.wait_ge(S[f"ssc{pp % 4}"], 16 * (pp // 4 + 1))
                            pidx = piece0 + kg
                            ins = v.tensor_tensor(wdq[c % 2][:, k * CW:(k + 1) * CW],
                                                  w8b[c % 2][:, k * CW:(k + 1) * CW],
                                                  scb[pidx % 4][:, (k % 4) * CW:(k % 4 + 1) * CW],
                                                  ALU.mult)
                            if k % 4 == 3:
                                ins.then_inc(S["e_dqp"], 1)
                    ops["vector"].append(v_dq)
                    C["e_dqp"] += 8

                    if c >= 1:
                        def v_ev(v, cprev=c - 1):
                            emit_evacs(v, cprev)
                        ops["vector"].append(v_ev)
                        C["e_ev"] += NTILE

                    def t_ch(t, c=c, dq_need=8 * (dqc_base + c + 1),
                             wo_sem=f"swot{c % 2}", wo_need=16 * ((half * NCH + c) // 2 + 1),
                             ev_b=ev_base, retr_ready=retr_ready):
                        if c == 0:
                            t.wait_ge(S["e_vretr"], retr_ready)
                        t.wait_ge(S["e_dqp"], dq_need)
                        if not nowo:
                            t.wait_ge(S[wo_sem], wo_need)
                        for tt in range(NTILE):
                            gi = c * NTILE + tt
                            if gi >= 4:
                                t.wait_ge(S["e_ev"], ev_b + gi - 3)
                            psq = ps_b4[tt % 4]
                            for k in range(KT):
                                t.matmul(psq[:, 0:CW],
                                         xnT[:, k * HALF + tt * 128: k * HALF + (tt + 1) * 128],
                                         wdq[c % 2][:, k * CW:(k + 1) * CW],
                                         start=(k == 0), stop=False)
                            if not (nowo or nowomm):
                                for g in range(H):
                                    t.matmul(psq[:, 0:CW],
                                             retrT[:, g * HALF + tt * 128: g * HALF + (tt + 1) * 128],
                                             wobf[c % 2][:, g * CW:(g + 1) * CW],
                                             start=False, stop=False)
                            t.matmul(psq[:, 0:CW], ones_sq[:, :], b_rep[:, c * CW:(c + 1) * CW],
                                     start=False, stop=True).then_inc(S["e_tf"], 1)
                    ops["tensor"].append(t_ch)
                    C["e_tf"] += NTILE

                def v_ev_last(v, cprev=NCH - 1):
                    emit_evacs(v, cprev)
                ops["vector"].append(v_ev_last)
                C["e_ev"] += NTILE

                def g_out_last(g, cprev=NCH - 1, half=half, ev_need=ev_base + 8 * NCH,
                               so=n_out):
                    g.wait_ge(S["e_ev"], ev_need)
                    if so >= 2:
                        g.wait_ge(S[f"sout{so % 2}"], 16 * (so // 2))
                    g.dma_start(
                        out=out.rearrange("(hh tt p) f -> hh tt p f", hh=2, p=128)[half]
                               .rearrange("tt p f -> p tt f")[:, :, cprev * CW:(cprev + 1) * CW],
                        in_=ost[cprev % 2][:, :].rearrange("p (tt j) -> p tt j", tt=NTILE),
                    ).then_inc(S[f"sout{so % 2}"], 16)
                ops["gpsimd"].append(g_out_last)
                n_out += 1

                emit(ops)
            barrier()

        if n_out:
            ops = mkops()
            t0v = 16 * ((n_out + 1) // 2)
            t1v = 16 * (n_out // 2)
            def g_fin(g, t0v=t0v, t1v=t1v):
                g.wait_ge(S["sout0"], t0v)
                g.wait_ge(S["sout1"], t1v)
            ops["gpsimd"].append(g_fin)
            emit(ops)

    return nc


_NC_CACHE = {}


def kernel(L0_attn_hidden_states, L0_attn_Wq, L0_attn_bq, L0_attn_Wk, L0_attn_bk,
           L0_attn_Wv, L0_attn_bv, L0_attn_Wo, L0_attn_M, L0_attn_z,
           L0_ffn_w, L0_ffn_w_scale, L0_ffn_b):
    x = np.ascontiguousarray(np.asarray(L0_attn_hidden_states, dtype=np.float32))
    Wq = np.asarray(L0_attn_Wq, dtype=np.float32)
    bq = np.asarray(L0_attn_bq, dtype=np.float32)
    Wo = np.asarray(L0_attn_Wo, dtype=np.float32)
    M = np.ascontiguousarray(np.asarray(L0_attn_M, dtype=np.float32))
    z = np.ascontiguousarray(np.asarray(L0_attn_z, dtype=np.float32))
    w8 = np.ascontiguousarray(np.asarray(L0_ffn_w, dtype=np.int8))
    wsc = np.ascontiguousarray(np.asarray(L0_ffn_w_scale, dtype=np.float32))
    bff = np.ascontiguousarray(np.asarray(L0_ffn_b, dtype=np.float32).reshape(1, D))

    Wq_sel = Wq.reshape(8, 4, 128, D)[:, 0].reshape(8 * 128, D)
    wqT = np.ascontiguousarray(Wq_sel.T)
    bq2 = np.ascontiguousarray(bq.reshape(8, 4, 128)[:, 0])
    woTb = np.ascontiguousarray(
        Wo.T.reshape(8, 4, 128, D).transpose(1, 0, 2, 3).reshape(4, 1024, D))
    ident = np.eye(128, dtype=np.float32)

    if "nc" not in _NC_CACHE:
        _NC_CACHE["nc"] = build_program()
    nc = _NC_CACHE["nc"]

    in_maps = []
    for i in range(8):
        in_maps.append(dict(x_in=np.ascontiguousarray(x[i]), wqT=wqT, bq2=bq2,
                            m_in=M, z_in=z, woTb=woTb, w8=w8, wsc=wsc, bff=bff,
                            ident=ident))
    res = run_bass_kernel_spmd(nc, in_maps, core_ids=list(range(8)))
    return np.stack([r["out"] for r in res.results], axis=0)
